# version 29
# baseline (speedup 1.0000x reference)
import numpy as np
import ml_dtypes
import contextlib
import concourse.bass as bass
import concourse.mybir as mybir
from concourse.bass_utils import run_bass_kernel_spmd

F32 = mybir.dt.float32
BF16 = mybir.dt.bfloat16
F8 = mybir.dt.float8e4
PM = mybir.MatmulPerfMode
AF = mybir.ActivationFunctionType
ALU = mybir.AluOpType
AX = mybir.AxisListType

T = 1024
NT = 8
NK = 16
NH = 16
NG = T // 8
NB = NG // 4
CSC = 1.0 / np.sqrt(128.0)

ENGS = ("sync", "gpsimd", "vector", "scalar", "tensor")


class Sem:
    def __init__(self, h):
        self.h = h
        self.n = 0

    def inc(self, k=1):
        self.n += k
        return self.n


class _Cut(Exception):
    pass


def build_program(ncut=4):
    nc = bass.Bass("TRN2", target_bir_lowering=False)

    x_in = nc.dram_tensor("x", [T, 2048], F32, kind="ExternalInput").ap()
    wqkv = nc.dram_tensor("wqkv", [2048, 6144], F32, kind="ExternalInput").ap()
    wout = nc.dram_tensor("wout", [2048, 2048], F32, kind="ExternalInput").ap()
    bout = nc.dram_tensor("bout", [1, 2048], F32, kind="ExternalInput").ap()
    in2 = nc.dram_tensor("in2", [2048, 2048], F32, kind="ExternalInput").ap()
    sc = nc.dram_tensor("sc", [1, 1], F32, kind="ExternalInput").ap()
    sc1 = nc.dram_tensor("sc1", [1, 1], F32, kind="ExternalInput").ap()
    id32 = nc.dram_tensor("id32", [128, 128], F32, kind="ExternalInput").ap()
    idbf = nc.dram_tensor("idbf", [128, 128], BF16, kind="ExternalInput").ap()
    maskb = nc.dram_tensor("maskb", [128, 128], BF16, kind="ExternalInput").ap()
    y_out = nc.dram_tensor("y", [T, 2048], F32, kind="ExternalOutput").ap()

    ops = {e: [] for e in ENGS}

    def emit(eng, fn):
        ops[eng].append(fn)

    es = contextlib.ExitStack()
    with es:
        ec = es.enter_context
        AB = ec(nc.sbuf_tensor("AB", [128, 16384], BF16))
        QT = ec(nc.sbuf_tensor("QT", [128, 16384], BF16))
        KV1 = ec(nc.sbuf_tensor("KV1", [128, 16384], BF16))
        VD = ec(nc.sbuf_tensor("VD", [128, 16384], BF16))
        Q1T = ec(nc.sbuf_tensor("Q1T", [128, 16384], F8))
        WQ = ec(nc.sbuf_tensor("WQ", [128, 8192], BF16))
        SCR32 = ec(nc.sbuf_tensor("SCR32", [128, 7680], F32))
        SCRBF = ec(nc.sbuf_tensor("SCRBF", [128, 8192], BF16))
        id32t = ec(nc.sbuf_tensor("id32t", [128, 128], F32))
        idbft = ec(nc.sbuf_tensor("idbft", [128, 128], BF16))
        maskt = ec(nc.sbuf_tensor("maskt", [128, 128], BF16))
        epsc = ec(nc.sbuf_tensor("epsc", [128, 1], F32))
        scol = ec(nc.sbuf_tensor("scol", [128, 1], F32))
        s1col = ec(nc.sbuf_tensor("s1col", [128, 1], F32))
        ccol = ec(nc.sbuf_tensor("ccol", [128, 1], F32))
        ctmp = ec(nc.sbuf_tensor("ctmp", [128, 2], F32))
        stat = ec(nc.sbuf_tensor("stat", [128, 32], F32))
        rscf = ec(nc.sbuf_tensor("rscf", [128, 32], F32))

        B0 = ec(nc.psum_tensor("B0", [128, 1024], F32))
        B1 = ec(nc.psum_tensor("B1", [128, 1024], F32))
        X0 = ec(nc.psum_tensor("X0", [128, 512], F32))
        X1 = ec(nc.psum_tensor("X1", [128, 512], F32))
        PVP = ec(nc.psum_tensor("PVP", [128, 1024], BF16))
        PWT = ec(nc.psum_tensor("PWT", [128, 1024], BF16))
        PB = [B0.ap()[:, 0:512], B0.ap()[:, 512:1024], X0.ap()[:], X1.ap()[:]]

        sC = Sem(ec(nc.semaphore("sC")))
        sX = [Sem(ec(nc.semaphore(f"sX{i}"))) for i in range(4)]
        sIN = [Sem(ec(nc.semaphore(f"sIN{i}"))) for i in range(2)]
        sXR = [Sem(ec(nc.semaphore(f"sXR{i}"))) for i in range(2)]
        sST = [Sem(ec(nc.semaphore(f"sST{i}"))) for i in range(3)]
        sWQ = [[Sem(ec(nc.semaphore(f"sWQ{i}{h}"))) for h in range(2)] for i in range(2)]
        sWO = [[Sem(ec(nc.semaphore(f"sWO{i}{h}"))) for h in range(2)] for i in range(2)]
        sBR = Sem(ec(nc.semaphore("sBR")))
        sT = Sem(ec(nc.semaphore("sT")))
        sV = Sem(ec(nc.semaphore("sV")))
        sS = Sem(ec(nc.semaphore("sS")))
        sG = Sem(ec(nc.semaphore("sG")))
        ESEM = {"vector": sV, "scalar": sS, "tensor": sT, "gpsimd": sG}

        last_w = {}
        readers = {}
        seen = {e: {} for e in ENGS}

        def _wait(eng, sem_, val):
            if val <= 0 or seen[eng].get(sem_, 0) >= val:
                return
            seen[eng][sem_] = val
            emit(eng, lambda e, s=sem_.h, v=val: e.wait_ge(s, v))

        def op(eng, fn, reads=(), writes=(), q=None):
            deps = {}
            for k in reads:
                w = last_w.get(k)
                if w:
                    deps[w[0]] = max(deps.get(w[0], 0), w[1])
            for k in writes:
                w = last_w.get(k)
                if w:
                    deps[w[0]] = max(deps.get(w[0], 0), w[1])
                for s_, v in readers.get(k, {}).items():
                    deps[s_] = max(deps.get(s_, 0), v)
            for s_, v in deps.items():
                _wait(eng, s_, v)
            sem_ = q if q is not None else ESEM[eng]
            amt = 16 if q is not None else 1
            emit(eng, lambda e, fn=fn, s=sem_.h, a=amt: fn(e).then_inc(s, a))
            t = sem_.inc(amt)
            tok = (sem_, t)
            for k in reads:
                readers.setdefault(k, {})[sem_] = t
            for k in writes:
                last_w[k] = tok
                readers[k] = {}
            return tok

        VDf = VD.ap().bitcast(F32)
        K1f = KV1.ap().bitcast(F32)
        xt_sl = [VDf[:, 0:2048], VDf[:, 2048:4096],
                 K1f[:, 0:2048], K1f[:, 2048:4096]]
        xn_sl = [VDf[:, 4096:6144], VDf[:, 6144:8192],
                 K1f[:, 4096:6144], K1f[:, 6144:8192]]
        XT_K = [[("xt", p)] for p in range(4)]
        XN_K = [[("xn", p)] for p in range(4)]
        BF_K = [("bf", i) for i in range(8)]

        def _phases():
         op("sync", lambda e: e.dma_start(out=id32t[:], in_=id32), writes=[("id32",)], q=sC)
         op("sync", lambda e: e.dma_start(out=idbft[:], in_=idbf), writes=[("idbf",)], q=sC)
         op("sync", lambda e: e.dma_start(out=maskt[:], in_=maskb), writes=[("maskt",)], q=sC)
         op("sync", lambda e: e.dma_start(out=scol[:], in_=sc.partition_broadcast(128)),
            writes=[("scol",)], q=sC)
         op("sync", lambda e: e.dma_start(out=s1col[:], in_=sc1.partition_broadcast(128)),
            writes=[("s1col",)], q=sC)
         for _k in (("id32",), ("idbf",), ("maskt",), ("scol",), ("s1col",)):
             last_w[_k] = (sC, sC.n)
         op("vector", lambda e: e.memset(epsc[:], 1e-5), writes=[("epsc",)])
         op("vector", lambda e: e.reciprocal(ctmp[:, 0:1], scol[:]),
            reads=[("scol",)], writes=[("ctmp",)])
         op("vector", lambda e: e.reciprocal(ctmp[:, 1:2], s1col[:]),
            reads=[("s1col",), ("ctmp",)], writes=[("ctmp",)])
         op("vector", lambda e: e.tensor_tensor(ccol[:], ctmp[:, 0:1], ctmp[:, 1:2], ALU.mult),
            reads=[("ctmp",)], writes=[("ccol",)])

         wq_slot = [WQ.ap()[:, 0:4096], WQ.ap()[:, 4096:8192]]

         def emit_wq(p):
             s = p % 2
             g = p % 24
             eng = "gpsimd"
             for h2 in range(2):
                 op(eng, lambda e, g=g, s=s, h2=h2: e.dma_start(
                     out=wq_slot[s].rearrange("q (k f) -> q k f", f=256)
                         [:, 8 * h2:8 * h2 + 8, :],
                     in_=wqkv.rearrange("(k q) (g f) -> q k g f", q=128, f=256)
                         [:, 8 * h2:8 * h2 + 8, g, :]),
                    writes=[("wq", s, h2)], q=sWQ[s][h2])

         emit_wq(0)
         emit_wq(1)

         btile = SCR32.ap()[:, 3584:5632]
         op("scalar", lambda e: e.dma_start(
             out=btile, in_=bout.partition_broadcast(128)),
            writes=[("bt",)], q=sBR)

         def p1_load_stats(tt):
             p = tt % 4
             op(("sync", "scalar")[tt % 2],
                lambda e, p=p, tt=tt: e.dma_start(
                    out=xt_sl[p], in_=x_in[tt * 128:(tt + 1) * 128, :]),
                writes=XT_K[p], q=sX[p])

             st = stat.ap()[:, 8 * p:8 * p + 8]
             msum, m_, m2, varep, sq_, rstd, ssq = [st[:, i:i + 1] for i in range(7)]
             SK = [("stat", p)]

             op("vector", lambda e, p=p, msum=msum: e.reduce_sum(msum, xt_sl[p], AX.X),
                reads=XT_K[p], writes=SK)
             op("scalar", lambda e, p=p, ssq=ssq: e.activation(
                 xn_sl[p], xt_sl[p], AF.Square, accum_out=ssq),
                reads=XT_K[p], writes=XN_K[p] + SK)
             op("vector", lambda e, msum=msum, m_=m_: e.tensor_scalar(
                 m_, msum, 1.0 / 2048.0, None, ALU.mult), reads=SK, writes=SK)
             op("vector", lambda e, msum=msum, m2=m2: e.tensor_scalar(
                 m2, msum, msum, 1.0 / 2048.0 ** 2, ALU.mult, ALU.mult),
                reads=SK, writes=SK)
             op("vector", lambda e, ssq=ssq, m2=m2, varep=varep: e.scalar_tensor_tensor(
                 varep, ssq, 1.0 / 2048.0, m2, ALU.mult, ALU.subtract),
                reads=SK, writes=SK)
             op("scalar", lambda e, varep=varep, sq_=sq_: e.activation(
                 sq_, varep, AF.Sqrt, bias=epsc[:]), reads=SK + [("epsc",)], writes=SK)
             op("vector", lambda e, sq_=sq_, rstd=rstd: e.reciprocal(rstd, sq_),
                reads=SK, writes=SK)

         def p1_xn_transpose(tt):
             p = tt % 4
             st = stat.ap()[:, 8 * p:8 * p + 8]
             m_, rstd = st[:, 1:2], st[:, 5:6]
             SK = [("stat", p)]
             op("vector", lambda e, p=p, m_=m_, rstd=rstd: e.tensor_scalar(
                 xn_sl[p], xt_sl[p], m_, rstd, ALU.subtract, ALU.mult),
                reads=XT_K[p] + SK, writes=XN_K[p])

             half = tt // 4
             for j in range(4):
                 bank = PB[(tt * 4 + j) % 4]
                 bk = ("PT", (tt * 4 + j) % 4)

                 def trgroup(e, bank=bank, p=p, j=j):
                     inst = None
                     for i in range(4):
                         k = 4 * j + i
                         inst = e.transpose(bank[:, i * 128:(i + 1) * 128],
                                            xn_sl[p][:, k * 128:(k + 1) * 128], id32t[:])
                     return inst
                 op("tensor", trgroup, reads=XN_K[p] + [("id32",)], writes=[bk])

                 dst_xnT = AB.ap().rearrange("q (k t) -> q k t", k=16)[
                     :, 4 * j:4 * j + 4, tt * 128:(tt + 1) * 128]
                 src3 = bank.rearrange("q (k t) -> q k t", k=4)
                 dst_q1 = Q1T.ap().rearrange("q (k t) -> q k t", k=16)[
                     :, 4 * j:4 * j + 4, tt * 128:(tt + 1) * 128]
                 if tt % 2 == 0:
                     op("scalar", lambda e, dst=dst_xnT, src=src3: e.copy(dst, src),
                        reads=[bk], writes=[("ABh", half), ("rr",) + bk])
                     op("vector", lambda e, dst=dst_q1, bank=bank: e.tensor_scalar(
                         dst, bank.rearrange("q (k t) -> q k t", k=4), scol[:],
                         None, ALU.mult),
                        reads=[bk, ("scol",), ("rr",) + bk], writes=[("Q1h", half)])
                 else:
                     op("vector", lambda e, dst=dst_xnT, src=src3: e.tensor_copy(dst, src),
                        reads=[bk], writes=[("ABh", half), ("rr",) + bk])
                     op("scalar", lambda e, dst=dst_q1, bank=bank: e.activation(
                         dst, bank.rearrange("q (k t) -> q k t", k=4),
                         AF.Copy, scale=scol[:]),
                        reads=[bk, ("scol",), ("rr",) + bk], writes=[("Q1h", half)])

         p1_load_stats(0)
         p1_load_stats(1)
         for tt in range(4):
             p1_xn_transpose(tt)
             if tt + 2 < 8:
                 p1_load_stats(tt + 2)

         if ncut < 2:
             raise _Cut
         xnT3 = AB.ap().rearrange("q (k t) -> q k t", k=16)
         kv_view = {
             "k": KV1.ap().rearrange("q (g s H) -> q g s H", s=8, H=16),
             "v": VD.ap().rearrange("q (g s H) -> q g s H", s=8, H=16),
         }
         KV_KEYS = {("k", 0): ("K1A",), ("k", 1): ("K1B",),
                    ("v", 0): ("VDA",), ("v", 1): ("VDB",)}
         QTB = [[("QTb", b) for b in range(8 * hf, 8 * hf + 8)] for hf in range(2)]

         def p2_mm(ft, half):
             s = (ft // 2) % 2
             ftl = ft % 2
             bank = PB[(2 * ft + half) % 4]
             bk = ("PT", (2 * ft + half) % 4)

             def mmgroup(e, bank=bank, s=s, ftl=ftl, half=half):
                 inst = None
                 w3 = wq_slot[s].rearrange("q (k f) -> q k f", f=256)
                 for j in range(NK):
                     inst = e.matmul(
                         bank,
                         w3[:, j, ftl * 128:(ftl + 1) * 128],
                         xnT3[:, j, half * 512:(half + 1) * 512],
                         start=(j == 0), stop=(j == NK - 1))
                 return inst
             op("tensor", mmgroup,
                reads=[("wq", s, 0), ("wq", s, 1), ("ABh", half)], writes=[bk])

             if ft < 16:
                 h = ft
                 dstq = QT.ap().rearrange("q (g s H) -> q g s H", s=8, H=16)[
                     :, half * 64:(half + 1) * 64, :, h]
                 op("vector", lambda e, dstq=dstq, bank=bank: e.tensor_copy(
                     dstq, bank.rearrange("q (g s) -> q g s", s=8)),
                    reads=[bk], writes=QTB[half])
             else:
                 H = (ft - 16) % 16
                 which = "k" if ft < 32 else "v"
                 eng = "scalar"
                 dst = kv_view[which][:, half * 64:(half + 1) * 64, :, H]
                 wr = [KV_KEYS[(which, half)]]
                 if H == 0:
                     if which == "v" and half == 0:
                         wr += [("xt", 0), ("xt", 1)]
                     elif which == "v" and half == 1:
                         wr += [("xn", 0), ("xn", 1)]
                     elif which == "k" and half == 0:
                         wr += [("xt", 2), ("xt", 3)]
                     else:
                         wr += [("xn", 2), ("xn", 3)]
                 op(eng, lambda e, dst=dst, bank=bank: e.copy(
                     dst, bank.rearrange("q (g s) -> q g s", s=8)),
                    reads=[bk], writes=wr)

         for ft in range(4):
             p1_xn_transpose(ft + 4)
             if ft + 6 < 8:
                 p1_load_stats(ft + 6)
             p2_mm(ft, 0)
             if ft % 2 == 1:
                 emit_wq(ft // 2 + 2)
         for ft in range(4, 48):
             p2_mm(ft, 0)
             p2_mm(ft, 1)
             if ft % 2 == 1 and ft // 2 + 2 < 24:
                 emit_wq(ft // 2 + 2)
         emit_wq(24)
         emit_wq(25)
         for ft in range(4):
             s = (ft // 2) % 2
             ftl = ft % 2
             bank = PB[(2 * ft + 1) % 4]
             bk = ("PT", (2 * ft + 1) % 4)

             def mmgroup(e, bank=bank, s=s, ftl=ftl):
                 inst = None
                 w3 = wq_slot[s].rearrange("q (k f) -> q k f", f=256)
                 for j in range(NK):
                     inst = e.matmul(
                         bank,
                         w3[:, j, ftl * 128:(ftl + 1) * 128],
                         xnT3[:, j, 512:1024],
                         start=(j == 0), stop=(j == NK - 1))
                 return inst
             op("tensor", mmgroup,
                reads=[("wq", s, 0), ("wq", s, 1), ("ABh", 1)], writes=[bk])
             dstq = QT.ap().rearrange("q (g s H) -> q g s H", s=8, H=16)[
                 :, 64:128, :, ft]
             op("vector", lambda e, dstq=dstq, bank=bank: e.tensor_copy(
                 dstq, bank.rearrange("q (g s) -> q g s", s=8)),
                reads=[bk], writes=QTB[1])

         if ncut < 3:
             raise _Cut
         NB2 = 16
         wm_sl = [SCRBF.ap()[:, 0:1024], SCRBF.ap()[:, 1024:2048]]
         wmn_sl = [SCRBF.ap()[:, 2048:3072], SCRBF.ap()[:, 3072:4096]]
         vp_sl = [SCRBF.ap()[:, 4096:5120], SCRBF.ap()[:, 5120:6144]]
         wmt_sl = [SCRBF.ap()[:, 6144:7168], SCRBF.ap()[:, 7168:8192]]
         WM_K = [("bf", 0), ("bf", 1)]
         WMN_K = [("bf", 2), ("bf", 3)]
         VP_K = [("bf", 4), ("bf", 5)]
         WMT_K = [("bf", 6), ("bf", 7)]
         mask_b = maskt.ap().unsqueeze(1).broadcast_to((128, 8, 128))
         SCK = [("PT", 0), ("PT", 1)]

         def p3_tensor_front(b):
             pb = b % 2

             def vpgroup(e, b=b):
                 inst = None
                 for i in range(8):
                     g = 8 * b + i
                     inst = e.transpose(PVP.ap()[:, i * 128:(i + 1) * 128],
                                        VD.ap()[:, g * 128:(g + 1) * 128], idbft[:])
                 return inst
             vk = [("VDA",) if b < 8 else ("VDB",), ("idbf",)]
             op("tensor", vpgroup, reads=vk, writes=[("PV",)])

             def scgroup(e, b=b):
                 inst = None
                 for i in range(8):
                     g = 8 * b + i
                     inst = e.matmul(B0.ap()[:, i * 128:(i + 1) * 128],
                                     QT.ap()[:, g * 128:(g + 1) * 128],
                                     KV1.ap()[:, g * 128:(g + 1) * 128],
                                     start=True, stop=True)
                 return inst
             op("tensor", scgroup,
                reads=[("K1A",) if b < 8 else ("K1B",), ("QTb", b)], writes=SCK)

         def p3_expmask(b):
             pb = b % 2
             if pb == 0:
                 op("vector", lambda e, pb=pb: e.tensor_copy(vp_sl[pb], PVP.ap()[:]),
                    reads=[("PV",)], writes=[VP_K[pb]])
             else:
                 op("scalar", lambda e, pb=pb: e.copy(vp_sl[pb], PVP.ap()[:]),
                    reads=[("PV",)], writes=[VP_K[pb]])
             op("scalar", lambda e, pb=pb: e.activation(
                 wm_sl[pb], B0.ap()[:], AF.Exp, scale=CSC),
                reads=SCK, writes=[WM_K[pb]])
             op("gpsimd", lambda e, pb=pb: e.tensor_tensor(
                 wm_sl[pb].rearrange("q (i m) -> q i m", i=8),
                 wm_sl[pb].rearrange("q (i m) -> q i m", i=8),
                 mask_b, ALU.mult),
                reads=[WM_K[pb], ("maskt",)], writes=[WM_K[pb]])

         def p3_consume(b):
             pb = b % 2
             dn = rscf.ap()[:, 16 * pb:16 * pb + 8]
             rq = rscf.ap()[:, 16 * pb + 8:16 * pb + 16]

             op("vector", lambda e, pb=pb, dn=dn: e.reduce_sum(
                 dn, wm_sl[pb].rearrange("q (i m) -> q i m", i=8), AX.X),
                reads=[WM_K[pb]], writes=[("dn", pb)])
             op("vector", lambda e, dn=dn, rq=rq: e.reciprocal(rq, dn),
                reads=[("dn", pb)], writes=[("rq", pb)])
             op("vector", lambda e, pb=pb, rq=rq: e.tensor_tensor(
                 wmn_sl[pb].rearrange("q (i m) -> q i m", i=8),
                 wm_sl[pb].rearrange("q (i m) -> q i m", i=8),
                 rq.unsqueeze(2).broadcast_to((128, 8, 128)), ALU.mult),
                reads=[WM_K[pb], ("rq", pb)], writes=[WMN_K[pb]])

             def wtgroup(e, pb=pb):
                 inst = None
                 for i in range(8):
                     inst = e.transpose(PWT.ap()[:, i * 128:(i + 1) * 128],
                                        wmn_sl[pb][:, i * 128:(i + 1) * 128], idbft[:])
                 return inst
             op("tensor", wtgroup, reads=[WMN_K[pb], ("idbf",)], writes=[("PW",)])

             if pb == 0:
                 op("vector", lambda e, pb=pb: e.tensor_copy(wmt_sl[pb], PWT.ap()[:]),
                    reads=[("PW",)], writes=[WMT_K[pb]])
             else:
                 op("scalar", lambda e, pb=pb: e.copy(wmt_sl[pb], PWT.ap()[:]),
                    reads=[("PW",)], writes=[WMT_K[pb]])

             def atgroup(e, pb=pb):
                 inst = None
                 for i in range(8):
                     inst = e.matmul(B1.ap()[:, i * 128:(i + 1) * 128],
                                     vp_sl[pb][:, i * 128:(i + 1) * 128],
                                     wmt_sl[pb][:, i * 128:(i + 1) * 128],
                                     start=True, stop=True)
                 return inst
             op("tensor", atgroup, reads=[VP_K[pb], WMT_K[pb]], writes=[("PA",)])

             if pb == 0:
                 op("scalar", lambda e, b=b: e.copy(
                     QT.ap()[:, 1024 * b:1024 * b + 1024], B1.ap()[:]),
                    reads=[("PA",)], writes=[("QTb", b)])
             else:
                 op("vector", lambda e, b=b: e.tensor_copy(
                     QT.ap()[:, 1024 * b:1024 * b + 1024], B1.ap()[:]),
                    reads=[("PA",)], writes=[("QTb", b)])

         w4buf = [WQ.ap()[:, 0:8192], KV1.ap()[:, 0:8192],
                  KV1.ap()[:, 8192:16384], WQ.ap()[:, 0:8192]]
         W4CLAIM = [[("wq", 0, 0), ("wq", 0, 1), ("wq", 1, 0), ("wq", 1, 1)],
                    [("K1A",)], [("K1B",)], [("w4", 0)]]
         q2r = [VD.ap()[:, 4096 * i:4096 * (i + 1)].bitcast(F8)[:, 0:8192]
                for i in range(4)]
         Q2CLAIM = [("VDA",), ("VDA",), ("VDB",), ("VDB",)]

         def emit_wout(ncn):
             op("gpsimd", lambda e, ncn=ncn: e.dma_start(
                 out=w4buf[ncn].rearrange("q (k f) -> q k f", f=512),
                 in_=wout.rearrange("(k q) (n f) -> q k n f", q=128, f=512)
                     [:, :, ncn, :]),
                writes=[("w4", ncn)] + W4CLAIM[ncn], q=sWO[ncn % 2][ncn // 2])

         i2_sl = [SCR32.ap()[:, 5632:6656], SCR32.ap()[:, 6656:7680]]

         def emit_in2_piece(ncn, i):
             s = i % 2
             op("sync", lambda e, ncn=ncn, i=i, s=s: e.dma_start(
                 out=i2_sl[s].rearrange("q (k f) -> q k f", f=512),
                 in_=in2.rearrange("(k q) (n f) -> q k n f", q=128, f=512)
                     [:, 2 * i:2 * i + 2, ncn, :]),
                writes=[("i2p", s)], q=sIN[s])
             q23 = q2r[ncn].rearrange("q (k f) -> q k f", f=512)
             eng = "scalar" if i % 2 == 0 else "vector"
             if eng == "scalar":
                 op(eng, lambda e, i=i, s=s, q23=q23: e.activation(
                     q23[:, 2 * i:2 * i + 2, :],
                     i2_sl[s].rearrange("q (k f) -> q k f", f=512),
                     AF.Copy, scale=s1col[:]),
                    reads=[("i2p", s), ("s1col",)],
                    writes=[("VDq", ncn), Q2CLAIM[ncn]])
             else:
                 op(eng, lambda e, i=i, s=s, q23=q23: e.tensor_scalar(
                     q23[:, 2 * i:2 * i + 2, :],
                     i2_sl[s].rearrange("q (k f) -> q k f", f=512),
                     s1col[:], None, ALU.mult),
                    reads=[("i2p", s), ("s1col",)],
                    writes=[("VDq", ncn), Q2CLAIM[ncn]])

         out_sl = [SCR32.ap()[:, i * 512:(i + 1) * 512] for i in range(3)]
         OUT_K = [("o32", i) for i in range(3)]
         tmp_sl = [SCR32.ap()[:, 1536:2048], SCR32.ap()[:, 2048:2560]]
         TMP_K = [("tmp", 0), ("tmp", 1)]
         xre_sl = [SCR32.ap()[:, 2560:3072], SCR32.ap()[:, 3072:3584]]
         XRE_K = [("xre", 0), ("xre", 1)]

         at3 = QT.ap().rearrange("q (t h) -> q h t", h=16)
         q13 = Q1T.ap().rearrange("q (k t) -> q k t", k=16)

         def p4_tile(ncn, tt, bA, bF, bAk, bFk, pj, oi, tail=None, xring="sync"):
             op(xring, lambda e, tt=tt, ncn=ncn, pj=pj: e.dma_start(
                 out=xre_sl[pj],
                 in_=x_in[tt * 128:(tt + 1) * 128, ncn * 512:(ncn + 1) * 512]),
                writes=[XRE_K[pj]], q=sXR[pj])
             op("gpsimd", lambda e, ncn=ncn, pj=pj: e.tensor_tensor(
                 xre_sl[pj], xre_sl[pj],
                 btile[:, ncn * 512:(ncn + 1) * 512], ALU.add),
                reads=[XRE_K[pj], ("bt",)], writes=[XRE_K[pj]])

             def agroup(e, bA=bA, tt=tt, ncn=ncn):
                 inst = None
                 w43 = w4buf[ncn].rearrange("q (k n) -> q k n", n=512)
                 for k in range(NK):
                     inst = e.matmul(bA,
                                     at3[:, k, tt * 128:(tt + 1) * 128],
                                     w43[:, k, :],
                                     start=(k == 0), stop=(k == NK - 1))
                 return inst
             op("tensor", agroup,
                reads=[("QTb", 2 * tt), ("QTb", 2 * tt + 1), ("w4", ncn)],
                writes=[bAk])

             def fgroup(e, bF=bF, tt=tt, ncn=ncn):
                 inst = None
                 q23 = q2r[ncn].rearrange("p (k n) -> p k n", n=512)
                 for j2 in range(NK // 2):
                     inst = e.matmul(bF,
                                     q13[:, 2 * j2:2 * j2 + 2, tt * 128:(tt + 1) * 128],
                                     q23[:, 2 * j2:2 * j2 + 2, :],
                                     start=(j2 == 0), stop=(j2 == NK // 2 - 1),
                                     perf_mode=PM.DoubleRow)
                 return inst
             op("tensor", fgroup, reads=[("Q1h", tt // 4), ("VDq", ncn)], writes=[bFk])

             def epi():
                 osl = out_sl[oi]
                 op("vector", lambda e, pj=pj, bF=bF: e.scalar_tensor_tensor(
                     tmp_sl[pj], bF, ccol[:], xre_sl[pj],
                     ALU.mult, ALU.add),
                    reads=[bFk, ("ccol",), XRE_K[pj]], writes=[TMP_K[pj]])
                 op("vector", lambda e, osl=osl, pj=pj, bA=bA: e.tensor_tensor(
                     osl, tmp_sl[pj], bA, ALU.add),
                    reads=[TMP_K[pj], bAk], writes=[OUT_K[oi]])
                 op("scalar", lambda e, osl=osl, tt=tt, ncn=ncn: e.dma_start(
                     out=y_out[tt * 128:(tt + 1) * 128, ncn * 512:(ncn + 1) * 512],
                     in_=osl), reads=[OUT_K[oi]], q=sST[oi])
             if tail is None:
                 epi()
             else:
                 tail.append(epi)

         p4tail = []
         emit_wout(0)
         op("sync", lambda e: e.dma_start(
             out=btile, in_=bout.partition_broadcast(128)),
            writes=[("bt",)], q=sBR)

         stg = ncut == 31 or ncut == 4
         ilv = ncut == 32 or ncut == 4
         for b in range(NB2):
             p3_tensor_front(b)
             p3_expmask(b)
             if stg and 4 <= b < 12:
                 ncn_p = (b - 4) // 4
                 emit_in2_piece(ncn_p, 2 * (b - 4 * ncn_p - 4))
                 emit_in2_piece(ncn_p, 2 * (b - 4 * ncn_p - 4) + 1)
             if stg and 12 <= b < 16:
                 emit_in2_piece(2, 2 * (b - 12))
                 emit_in2_piece(2, 2 * (b - 12) + 1)
             if stg and b == 8:
                 emit_wout(1)
             if b > 0:
                 p3_consume(b - 1)
             if ilv and 10 <= b:
                 t = b - 10
                 while p4tail:
                     p4tail.pop(0)()
                 p4_tile(0, t, PB[2], PB[3], ("PT", 2), ("PT", 3), t % 2, t % 3,
                         tail=p4tail, xring="scalar")
         p3_consume(NB2 - 1)
         while p4tail:
             p4tail.pop(0)()
         if ilv:
             p4_tile(0, 6, PB[2], PB[3], ("PT", 2), ("PT", 3), 0, 0, xring="scalar")
             p4_tile(0, 7, PB[2], PB[3], ("PT", 2), ("PT", 3), 1, 1, xring="scalar")
         if stg:
             emit_wout(2)

         if ncut != 4:
             raise _Cut
         for ncn in range((1 if ilv else 0), 4):
             for tt in range(NT):
                 j = ncn * NT + tt
                 pj = j % 2
                 p4_tile(ncn, tt, PB[2 * pj], PB[1 + 2 * pj],
                         ("PT", 2 * pj), ("PT", 1 + 2 * pj), pj, j % 3)
                 if ncn == 1 and tt < 8:
                     emit_in2_piece(3, tt)
             if ncn == 1:
                 emit_wout(3)

         for i in range(3):
             emit("sync", lambda e, s_=sST[i].h, v=sST[i].n: e.wait_ge(s_, v))

        try:
            _phases()
        except _Cut:
            op("gpsimd", lambda e: e.dma_start(out=y_out[0:128, :],
               in_=AB.ap().bitcast(F32)[:, 0:2048]), reads=[("ABh", 0)], q=sBR)
            emit("sync", lambda e, s_=sBR.h, v=sBR.n: e.wait_ge(s_, v))

        block = ec(nc.Block())

        @block.sync
        def _(e):
            for f in ops["sync"]:
                f(e)

        @block.gpsimd
        def _(e):
            for f in ops["gpsimd"]:
                f(e)

        @block.vector
        def _(e):
            for f in ops["vector"]:
                f(e)

        @block.scalar
        def _(e):
            for f in ops["scalar"]:
                f(e)

        @block.tensor
        def _(e):
            for f in ops["tensor"]:
                f(e)

    return nc


_NC_CACHE = None


def kernel(L0_attn_x, L0_attn_W_qkv, L0_attn_W_out, L0_attn_b_out,
           L0_ffn_input2, L0_ffn_scale, L0_ffn_scale1):
    global _NC_CACHE
    if _NC_CACHE is None:
        _NC_CACHE = build_program()
    nc = _NC_CACHE

    x = np.ascontiguousarray(np.asarray(L0_attn_x, dtype=np.float32).reshape(8192, 2048))
    wqkv = np.ascontiguousarray(np.asarray(L0_attn_W_qkv, dtype=np.float32))
    wout = np.ascontiguousarray(np.asarray(L0_attn_W_out, dtype=np.float32))
    bout = np.ascontiguousarray(np.asarray(L0_attn_b_out, dtype=np.float32).reshape(1, 2048))
    in2 = np.ascontiguousarray(np.asarray(L0_ffn_input2, dtype=np.float32))
    sc = np.asarray(L0_ffn_scale, dtype=np.float32).reshape(1, 1)
    sc1 = np.asarray(L0_ffn_scale1, dtype=np.float32).reshape(1, 1)

    id32 = np.eye(128, dtype=np.float32)
    idbf = np.eye(128, dtype=ml_dtypes.bfloat16)
    maskb = np.kron(np.eye(8, dtype=np.float32),
                    np.ones((16, 16), dtype=np.float32)).astype(ml_dtypes.bfloat16)

    in_maps = []
    for c in range(8):
        in_maps.append({
            "x": x[c * T:(c + 1) * T],
            "wqkv": wqkv, "wout": wout, "bout": bout, "in2": in2,
            "sc": sc, "sc1": sc1,
            "id32": id32, "idbf": idbf, "maskb": maskb,
        })

    res = run_bass_kernel_spmd(nc, in_maps, core_ids=list(range(8)))
    y = np.concatenate([r["y"] for r in res.results], axis=0)
    return y.reshape(4, 2048, 2048)


# version 30
# speedup vs baseline: 1.0136x; 1.0136x over previous
import numpy as np
import ml_dtypes
import contextlib
import concourse.bass as bass
import concourse.mybir as mybir
from concourse.bass_utils import run_bass_kernel_spmd

F32 = mybir.dt.float32
BF16 = mybir.dt.bfloat16
F8 = mybir.dt.float8e4
PM = mybir.MatmulPerfMode
AF = mybir.ActivationFunctionType
ALU = mybir.AluOpType
AX = mybir.AxisListType

T = 1024
NT = 8
NK = 16
NH = 16
NG = T // 8
NB = NG // 4
CSC = 1.0 / np.sqrt(128.0)

ENGS = ("sync", "gpsimd", "vector", "scalar", "tensor")


class Sem:
    def __init__(self, h):
        self.h = h
        self.n = 0

    def inc(self, k=1):
        self.n += k
        return self.n


class _Cut(Exception):
    pass


def build_program(ncut=4):
    nc = bass.Bass("TRN2", target_bir_lowering=False)

    x_in = nc.dram_tensor("x", [T, 2048], F32, kind="ExternalInput").ap()
    wqkv = nc.dram_tensor("wqkv", [2048, 6144], F32, kind="ExternalInput").ap()
    wout = nc.dram_tensor("wout", [2048, 2048], F32, kind="ExternalInput").ap()
    bout = nc.dram_tensor("bout", [1, 2048], F32, kind="ExternalInput").ap()
    in2 = nc.dram_tensor("in2", [2048, 2048], F32, kind="ExternalInput").ap()
    sc = nc.dram_tensor("sc", [1, 1], F32, kind="ExternalInput").ap()
    sc1 = nc.dram_tensor("sc1", [1, 1], F32, kind="ExternalInput").ap()
    id32 = nc.dram_tensor("id32", [128, 128], F32, kind="ExternalInput").ap()
    idbf = nc.dram_tensor("idbf", [128, 128], BF16, kind="ExternalInput").ap()
    maskb = nc.dram_tensor("maskb", [128, 128], BF16, kind="ExternalInput").ap()
    y_out = nc.dram_tensor("y", [T, 2048], F32, kind="ExternalOutput").ap()

    ops = {e: [] for e in ENGS}

    def emit(eng, fn):
        ops[eng].append(fn)

    es = contextlib.ExitStack()
    with es:
        ec = es.enter_context
        AB = ec(nc.sbuf_tensor("AB", [128, 16384], BF16))
        QT = ec(nc.sbuf_tensor("QT", [128, 16384], BF16))
        KV1 = ec(nc.sbuf_tensor("KV1", [128, 16384], BF16))
        VD = ec(nc.sbuf_tensor("VD", [128, 16384], BF16))
        Q1T = ec(nc.sbuf_tensor("Q1T", [128, 16384], F8))
        WQ = ec(nc.sbuf_tensor("WQ", [128, 8192], BF16))
        SCR32 = ec(nc.sbuf_tensor("SCR32", [128, 7680], F32))
        SCRBF = ec(nc.sbuf_tensor("SCRBF", [128, 8192], BF16))
        id32t = ec(nc.sbuf_tensor("id32t", [128, 128], F32))
        idbft = ec(nc.sbuf_tensor("idbft", [128, 128], BF16))
        maskt = ec(nc.sbuf_tensor("maskt", [128, 128], BF16))
        epsc = ec(nc.sbuf_tensor("epsc", [128, 1], F32))
        scol = ec(nc.sbuf_tensor("scol", [128, 1], F32))
        s1col = ec(nc.sbuf_tensor("s1col", [128, 1], F32))
        ccol = ec(nc.sbuf_tensor("ccol", [128, 1], F32))
        ctmp = ec(nc.sbuf_tensor("ctmp", [128, 2], F32))
        stat = ec(nc.sbuf_tensor("stat", [128, 32], F32))
        rscf = ec(nc.sbuf_tensor("rscf", [128, 32], F32))

        B0 = ec(nc.psum_tensor("B0", [128, 1024], F32))
        B1 = ec(nc.psum_tensor("B1", [128, 1024], F32))
        X0 = ec(nc.psum_tensor("X0", [128, 512], F32))
        X1 = ec(nc.psum_tensor("X1", [128, 512], F32))
        PVP = ec(nc.psum_tensor("PVP", [128, 1024], BF16))
        PWT = ec(nc.psum_tensor("PWT", [128, 1024], BF16))
        PB = [B0.ap()[:, 0:512], B0.ap()[:, 512:1024], X0.ap()[:], X1.ap()[:]]

        sC = Sem(ec(nc.semaphore("sC")))
        sX = [Sem(ec(nc.semaphore(f"sX{i}"))) for i in range(4)]
        sIN = [Sem(ec(nc.semaphore(f"sIN{i}"))) for i in range(2)]
        sXR = [Sem(ec(nc.semaphore(f"sXR{i}"))) for i in range(2)]
        sST = [Sem(ec(nc.semaphore(f"sST{i}"))) for i in range(3)]
        sWQ = [[Sem(ec(nc.semaphore(f"sWQ{i}{h}"))) for h in range(2)] for i in range(2)]
        sWO = [[Sem(ec(nc.semaphore(f"sWO{i}{h}"))) for h in range(2)] for i in range(2)]
        sBR = Sem(ec(nc.semaphore("sBR")))
        sT = Sem(ec(nc.semaphore("sT")))
        sV = Sem(ec(nc.semaphore("sV")))
        sS = Sem(ec(nc.semaphore("sS")))
        sG = Sem(ec(nc.semaphore("sG")))
        ESEM = {"vector": sV, "scalar": sS, "tensor": sT, "gpsimd": sG}

        last_w = {}
        readers = {}
        seen = {e: {} for e in ENGS}

        def _wait(eng, sem_, val):
            if val <= 0 or seen[eng].get(sem_, 0) >= val:
                return
            seen[eng][sem_] = val
            emit(eng, lambda e, s=sem_.h, v=val: e.wait_ge(s, v))

        def op(eng, fn, reads=(), writes=(), q=None):
            deps = {}
            for k in reads:
                w = last_w.get(k)
                if w:
                    deps[w[0]] = max(deps.get(w[0], 0), w[1])
            for k in writes:
                w = last_w.get(k)
                if w:
                    deps[w[0]] = max(deps.get(w[0], 0), w[1])
                for s_, v in readers.get(k, {}).items():
                    deps[s_] = max(deps.get(s_, 0), v)
            for s_, v in deps.items():
                _wait(eng, s_, v)
            sem_ = q if q is not None else ESEM[eng]
            amt = 16 if q is not None else 1
            emit(eng, lambda e, fn=fn, s=sem_.h, a=amt: fn(e).then_inc(s, a))
            t = sem_.inc(amt)
            tok = (sem_, t)
            for k in reads:
                readers.setdefault(k, {})[sem_] = t
            for k in writes:
                last_w[k] = tok
                readers[k] = {}
            return tok

        VDf = VD.ap().bitcast(F32)
        K1f = KV1.ap().bitcast(F32)
        xt_sl = [VDf[:, 0:2048], VDf[:, 2048:4096],
                 K1f[:, 0:2048], K1f[:, 2048:4096]]
        xn_sl = [VDf[:, 4096:6144], VDf[:, 6144:8192],
                 K1f[:, 4096:6144], K1f[:, 6144:8192]]
        XT_K = [[("xt", p)] for p in range(4)]
        XN_K = [[("xn", p)] for p in range(4)]
        BF_K = [("bf", i) for i in range(8)]

        def _phases():
         op("sync", lambda e: e.dma_start(out=id32t[:], in_=id32), writes=[("id32",)], q=sC)
         op("sync", lambda e: e.dma_start(out=idbft[:], in_=idbf), writes=[("idbf",)], q=sC)
         op("sync", lambda e: e.dma_start(out=maskt[:], in_=maskb), writes=[("maskt",)], q=sC)
         op("sync", lambda e: e.dma_start(out=scol[:], in_=sc.partition_broadcast(128)),
            writes=[("scol",)], q=sC)
         op("sync", lambda e: e.dma_start(out=s1col[:], in_=sc1.partition_broadcast(128)),
            writes=[("s1col",)], q=sC)
         for _k in (("id32",), ("idbf",), ("maskt",), ("scol",), ("s1col",)):
             last_w[_k] = (sC, sC.n)
         op("vector", lambda e: e.memset(epsc[:], 1e-5), writes=[("epsc",)])
         op("vector", lambda e: e.reciprocal(ctmp[:, 0:1], scol[:]),
            reads=[("scol",)], writes=[("ctmp",)])
         op("vector", lambda e: e.reciprocal(ctmp[:, 1:2], s1col[:]),
            reads=[("s1col",), ("ctmp",)], writes=[("ctmp",)])
         op("vector", lambda e: e.tensor_tensor(ccol[:], ctmp[:, 0:1], ctmp[:, 1:2], ALU.mult),
            reads=[("ctmp",)], writes=[("ccol",)])

         wq_slot = [WQ.ap()[:, 0:4096], WQ.ap()[:, 4096:8192]]

         def emit_wq(p):
             s = p % 2
             g = p % 24
             eng = "gpsimd"
             for h2 in range(2):
                 op(eng, lambda e, g=g, s=s, h2=h2: e.dma_start(
                     out=wq_slot[s].rearrange("q (k f) -> q k f", f=256)
                         [:, 8 * h2:8 * h2 + 8, :],
                     in_=wqkv.rearrange("(k q) (g f) -> q k g f", q=128, f=256)
                         [:, 8 * h2:8 * h2 + 8, g, :]),
                    writes=[("wq", s, h2)], q=sWQ[s][h2])

         emit_wq(0)
         emit_wq(1)

         btile = SCR32.ap()[:, 3584:5632]
         op("scalar", lambda e: e.dma_start(
             out=btile, in_=bout.partition_broadcast(128)),
            writes=[("bt",)], q=sBR)

         def p1_load_stats(tt):
             p = tt % 4
             op(("sync", "scalar")[tt % 2],
                lambda e, p=p, tt=tt: e.dma_start(
                    out=xt_sl[p], in_=x_in[tt * 128:(tt + 1) * 128, :]),
                writes=XT_K[p], q=sX[p])

             st = stat.ap()[:, 8 * p:8 * p + 8]
             msum, m_, m2, varep, sq_, rstd, ssq = [st[:, i:i + 1] for i in range(7)]
             SK = [("stat", p)]

             op("vector", lambda e, p=p, msum=msum: e.reduce_sum(msum, xt_sl[p], AX.X),
                reads=XT_K[p], writes=SK)
             op("scalar", lambda e, p=p, ssq=ssq: e.activation(
                 xn_sl[p], xt_sl[p], AF.Square, accum_out=ssq),
                reads=XT_K[p], writes=XN_K[p] + SK)
             op("vector", lambda e, msum=msum, m_=m_: e.tensor_scalar(
                 m_, msum, 1.0 / 2048.0, None, ALU.mult), reads=SK, writes=SK)
             op("vector", lambda e, msum=msum, m2=m2: e.tensor_scalar(
                 m2, msum, msum, 1.0 / 2048.0 ** 2, ALU.mult, ALU.mult),
                reads=SK, writes=SK)
             op("vector", lambda e, ssq=ssq, m2=m2, varep=varep: e.scalar_tensor_tensor(
                 varep, ssq, 1.0 / 2048.0, m2, ALU.mult, ALU.subtract),
                reads=SK, writes=SK)
             op("scalar", lambda e, varep=varep, sq_=sq_: e.activation(
                 sq_, varep, AF.Sqrt, bias=epsc[:]), reads=SK + [("epsc",)], writes=SK)
             op("vector", lambda e, sq_=sq_, rstd=rstd: e.reciprocal(rstd, sq_),
                reads=SK, writes=SK)

         def p1_xn_transpose(tt):
             p = tt % 4
             st = stat.ap()[:, 8 * p:8 * p + 8]
             m_, rstd = st[:, 1:2], st[:, 5:6]
             SK = [("stat", p)]
             op("vector", lambda e, p=p, m_=m_, rstd=rstd: e.tensor_scalar(
                 xn_sl[p], xt_sl[p], m_, rstd, ALU.subtract, ALU.mult),
                reads=XT_K[p] + SK, writes=XN_K[p])

             half = tt // 4
             for j in range(4):
                 bank = PB[(tt * 4 + j) % 4]
                 bk = ("PT", (tt * 4 + j) % 4)

                 def trgroup(e, bank=bank, p=p, j=j):
                     inst = None
                     for i in range(4):
                         k = 4 * j + i
                         inst = e.transpose(bank[:, i * 128:(i + 1) * 128],
                                            xn_sl[p][:, k * 128:(k + 1) * 128], id32t[:])
                     return inst
                 op("tensor", trgroup, reads=XN_K[p] + [("id32",)], writes=[bk])

                 dst_xnT = AB.ap().rearrange("q (k t) -> q k t", k=16)[
                     :, 4 * j:4 * j + 4, tt * 128:(tt + 1) * 128]
                 src3 = bank.rearrange("q (k t) -> q k t", k=4)
                 dst_q1 = Q1T.ap().rearrange("q (k t) -> q k t", k=16)[
                     :, 4 * j:4 * j + 4, tt * 128:(tt + 1) * 128]
                 if tt % 2 == 0:
                     op("scalar", lambda e, dst=dst_xnT, src=src3: e.copy(dst, src),
                        reads=[bk], writes=[("ABh", half), ("rr",) + bk])
                     op("vector", lambda e, dst=dst_q1, bank=bank: e.tensor_scalar(
                         dst, bank.rearrange("q (k t) -> q k t", k=4), scol[:],
                         None, ALU.mult),
                        reads=[bk, ("scol",), ("rr",) + bk], writes=[("Q1h", half)])
                 else:
                     op("vector", lambda e, dst=dst_xnT, src=src3: e.tensor_copy(dst, src),
                        reads=[bk], writes=[("ABh", half), ("rr",) + bk])
                     op("scalar", lambda e, dst=dst_q1, bank=bank: e.activation(
                         dst, bank.rearrange("q (k t) -> q k t", k=4),
                         AF.Copy, scale=scol[:]),
                        reads=[bk, ("scol",), ("rr",) + bk], writes=[("Q1h", half)])

         p1_load_stats(0)
         p1_load_stats(1)
         for tt in range(4):
             p1_xn_transpose(tt)
             if tt + 2 < 8:
                 p1_load_stats(tt + 2)

         if ncut < 2:
             raise _Cut
         xnT3 = AB.ap().rearrange("q (k t) -> q k t", k=16)
         kv_view = {
             "k": KV1.ap().rearrange("q (g s H) -> q g s H", s=8, H=16),
             "v": VD.ap().rearrange("q (g s H) -> q g s H", s=8, H=16),
         }
         KV_KEYS = {("k", 0): ("K1A",), ("k", 1): ("K1B",),
                    ("v", 0): ("VDA",), ("v", 1): ("VDB",)}
         QTB = [[("QTb", b) for b in range(8 * hf, 8 * hf + 8)] for hf in range(2)]

         def p2_mm(ft, half):
             s = (ft // 2) % 2
             ftl = ft % 2
             bank = PB[(2 * ft + half) % 4]
             bk = ("PT", (2 * ft + half) % 4)

             def mmgroup(e, bank=bank, s=s, ftl=ftl, half=half):
                 inst = None
                 w3 = wq_slot[s].rearrange("q (k f) -> q k f", f=256)
                 for j in range(NK):
                     inst = e.matmul(
                         bank,
                         w3[:, j, ftl * 128:(ftl + 1) * 128],
                         xnT3[:, j, half * 512:(half + 1) * 512],
                         start=(j == 0), stop=(j == NK - 1))
                 return inst
             op("tensor", mmgroup,
                reads=[("wq", s, 0), ("wq", s, 1), ("ABh", half)], writes=[bk])

             if ft < 16:
                 h = ft
                 dstq = QT.ap().rearrange("q (g s H) -> q g s H", s=8, H=16)[
                     :, half * 64:(half + 1) * 64, :, h]
                 op("vector", lambda e, dstq=dstq, bank=bank: e.tensor_copy(
                     dstq, bank.rearrange("q (g s) -> q g s", s=8)),
                    reads=[bk], writes=QTB[half])
             else:
                 H = (ft - 16) % 16
                 which = "k" if ft < 32 else "v"
                 eng = "scalar"
                 dst = kv_view[which][:, half * 64:(half + 1) * 64, :, H]
                 wr = [KV_KEYS[(which, half)]]
                 if H == 0:
                     if which == "v" and half == 0:
                         wr += [("xt", 0), ("xt", 1)]
                     elif which == "v" and half == 1:
                         wr += [("xn", 0), ("xn", 1)]
                     elif which == "k" and half == 0:
                         wr += [("xt", 2), ("xt", 3)]
                     else:
                         wr += [("xn", 2), ("xn", 3)]
                 op(eng, lambda e, dst=dst, bank=bank: e.copy(
                     dst, bank.rearrange("q (g s) -> q g s", s=8)),
                    reads=[bk], writes=wr)

         for ft in range(4):
             p1_xn_transpose(ft + 4)
             if ft + 6 < 8:
                 p1_load_stats(ft + 6)
             p2_mm(ft, 0)
             if ft % 2 == 1:
                 emit_wq(ft // 2 + 2)
         for ft in range(4, 48):
             p2_mm(ft, 0)
             p2_mm(ft, 1)
             if ft % 2 == 1 and ft // 2 + 2 < 24:
                 emit_wq(ft // 2 + 2)
         emit_wq(24)
         emit_wq(25)
         for ft in range(4):
             s = (ft // 2) % 2
             ftl = ft % 2
             bank = PB[(2 * ft + 1) % 4]
             bk = ("PT", (2 * ft + 1) % 4)

             def mmgroup(e, bank=bank, s=s, ftl=ftl):
                 inst = None
                 w3 = wq_slot[s].rearrange("q (k f) -> q k f", f=256)
                 for j in range(NK):
                     inst = e.matmul(
                         bank,
                         w3[:, j, ftl * 128:(ftl + 1) * 128],
                         xnT3[:, j, 512:1024],
                         start=(j == 0), stop=(j == NK - 1))
                 return inst
             op("tensor", mmgroup,
                reads=[("wq", s, 0), ("wq", s, 1), ("ABh", 1)], writes=[bk])
             dstq = QT.ap().rearrange("q (g s H) -> q g s H", s=8, H=16)[
                 :, 64:128, :, ft]
             op("vector", lambda e, dstq=dstq, bank=bank: e.tensor_copy(
                 dstq, bank.rearrange("q (g s) -> q g s", s=8)),
                reads=[bk], writes=QTB[1])

         if ncut < 3:
             raise _Cut
         NB2 = 16
         wm_sl = [SCRBF.ap()[:, i * 1024:(i + 1) * 1024] for i in range(3)]
         vp_sl = [SCRBF.ap()[:, 3072 + i * 1024:3072 + (i + 1) * 1024] for i in range(3)]
         wmn_sl = SCRBF.ap()[:, 6144:7168]
         wmt_sl = SCRBF.ap()[:, 7168:8192]
         WM_K = [("bf", i) for i in range(3)]
         VP_K = [("bf", 3 + i) for i in range(3)]
         WMN_K = ("wmn",)
         WMT_K = ("wmt",)
         mask_b = maskt.ap().unsqueeze(1).broadcast_to((128, 8, 128))
         SCB = [B0, B1]
         SCKs = [[("PT", 0), ("PT", 1)], [("B1",)]]

         def p3_tensor_front(b):
             pb = b % 2

             def vpgroup(e, b=b):
                 inst = None
                 for i in range(8):
                     g = 8 * b + i
                     inst = e.transpose(PVP.ap()[:, i * 128:(i + 1) * 128],
                                        VD.ap()[:, g * 128:(g + 1) * 128], idbft[:])
                 return inst
             vk = [("VDA",) if b < 8 else ("VDB",), ("idbf",)]
             op("tensor", vpgroup, reads=vk, writes=[("PV",)])

             def scgroup(e, b=b, pb=pb):
                 inst = None
                 for i in range(8):
                     g = 8 * b + i
                     inst = e.matmul(SCB[pb].ap()[:, i * 128:(i + 1) * 128],
                                     QT.ap()[:, g * 128:(g + 1) * 128],
                                     KV1.ap()[:, g * 128:(g + 1) * 128],
                                     start=True, stop=True)
                 return inst
             op("tensor", scgroup,
                reads=[("K1A",) if b < 8 else ("K1B",), ("QTb", b)], writes=SCKs[pb])

         def p3_expmask(b):
             pb = b % 2
             p3 = b % 3
             if pb == 0:
                 op("vector", lambda e, p3=p3: e.tensor_copy(vp_sl[p3], PVP.ap()[:]),
                    reads=[("PV",)], writes=[VP_K[p3]])
             else:
                 op("scalar", lambda e, p3=p3: e.copy(vp_sl[p3], PVP.ap()[:]),
                    reads=[("PV",)], writes=[VP_K[p3]])
             op("scalar", lambda e, pb=pb, p3=p3: e.activation(
                 wm_sl[p3], SCB[pb].ap()[:], AF.Exp, scale=CSC),
                reads=SCKs[pb], writes=[WM_K[p3]])
             op("gpsimd", lambda e, p3=p3: e.tensor_tensor(
                 wm_sl[p3].rearrange("q (i m) -> q i m", i=8),
                 wm_sl[p3].rearrange("q (i m) -> q i m", i=8),
                 mask_b, ALU.mult),
                reads=[WM_K[p3], ("maskt",)], writes=[WM_K[p3]])

         def p3_consume(b):
             pb = b % 2
             p3 = b % 3
             dn = rscf.ap()[:, 16 * pb:16 * pb + 8]
             rq = rscf.ap()[:, 16 * pb + 8:16 * pb + 16]

             op("vector", lambda e, p3=p3, dn=dn: e.reduce_sum(
                 dn, wm_sl[p3].rearrange("q (i m) -> q i m", i=8), AX.X),
                reads=[WM_K[p3]], writes=[("dn", pb)])
             op("vector", lambda e, dn=dn, rq=rq: e.reciprocal(rq, dn),
                reads=[("dn", pb)], writes=[("rq", pb)])
             op("vector", lambda e, p3=p3, rq=rq: e.tensor_tensor(
                 wmn_sl.rearrange("q (i m) -> q i m", i=8),
                 wm_sl[p3].rearrange("q (i m) -> q i m", i=8),
                 rq.unsqueeze(2).broadcast_to((128, 8, 128)), ALU.mult),
                reads=[WM_K[p3], ("rq", pb)], writes=[WMN_K])

             def wtgroup(e):
                 inst = None
                 for i in range(8):
                     inst = e.transpose(PWT.ap()[:, i * 128:(i + 1) * 128],
                                        wmn_sl[:, i * 128:(i + 1) * 128], idbft[:])
                 return inst
             op("tensor", wtgroup, reads=[WMN_K, ("idbf",)], writes=[("PW",)])

             if pb == 0:
                 op("vector", lambda e: e.tensor_copy(wmt_sl, PWT.ap()[:]),
                    reads=[("PW",)], writes=[WMT_K])
             else:
                 op("scalar", lambda e: e.copy(wmt_sl, PWT.ap()[:]),
                    reads=[("PW",)], writes=[WMT_K])

             def atgroup(e, pb=pb, p3=p3):
                 inst = None
                 for i in range(8):
                     inst = e.matmul(SCB[pb].ap()[:, i * 128:(i + 1) * 128],
                                     vp_sl[p3][:, i * 128:(i + 1) * 128],
                                     wmt_sl[:, i * 128:(i + 1) * 128],
                                     start=True, stop=True)
                 return inst
             op("tensor", atgroup, reads=[VP_K[p3], WMT_K], writes=SCKs[pb])

             if pb == 0:
                 op("scalar", lambda e, b=b, pb=pb: e.copy(
                     QT.ap()[:, 1024 * b:1024 * b + 1024], SCB[pb].ap()[:]),
                    reads=SCKs[pb], writes=[("QTb", b)])
             else:
                 op("vector", lambda e, b=b, pb=pb: e.tensor_copy(
                     QT.ap()[:, 1024 * b:1024 * b + 1024], SCB[pb].ap()[:]),
                    reads=SCKs[pb], writes=[("QTb", b)])

         w4buf = [WQ.ap()[:, 0:8192], KV1.ap()[:, 0:8192],
                  KV1.ap()[:, 8192:16384], WQ.ap()[:, 0:8192]]
         W4CLAIM = [[("wq", 0, 0), ("wq", 0, 1), ("wq", 1, 0), ("wq", 1, 1)],
                    [("K1A",)], [("K1B",)], [("w4", 0)]]
         q2r = [VD.ap()[:, 4096 * i:4096 * (i + 1)].bitcast(F8)[:, 0:8192]
                for i in range(4)]
         Q2CLAIM = [("VDA",), ("VDA",), ("VDB",), ("VDB",)]

         def emit_wout(ncn):
             op("gpsimd", lambda e, ncn=ncn: e.dma_start(
                 out=w4buf[ncn].rearrange("q (k f) -> q k f", f=512),
                 in_=wout.rearrange("(k q) (n f) -> q k n f", q=128, f=512)
                     [:, :, ncn, :]),
                writes=[("w4", ncn)] + W4CLAIM[ncn], q=sWO[ncn % 2][ncn // 2])

         i2_sl = [SCR32.ap()[:, 5632:6656], SCR32.ap()[:, 6656:7680]]

         def emit_in2_piece(ncn, i):
             s = i % 2
             op("sync", lambda e, ncn=ncn, i=i, s=s: e.dma_start(
                 out=i2_sl[s].rearrange("q (k f) -> q k f", f=512),
                 in_=in2.rearrange("(k q) (n f) -> q k n f", q=128, f=512)
                     [:, 2 * i:2 * i + 2, ncn, :]),
                writes=[("i2p", s)], q=sIN[s])
             q23 = q2r[ncn].rearrange("q (k f) -> q k f", f=512)
             eng = "scalar" if i % 2 == 0 else "vector"
             if eng == "scalar":
                 op(eng, lambda e, i=i, s=s, q23=q23: e.activation(
                     q23[:, 2 * i:2 * i + 2, :],
                     i2_sl[s].rearrange("q (k f) -> q k f", f=512),
                     AF.Copy, scale=s1col[:]),
                    reads=[("i2p", s), ("s1col",)],
                    writes=[("VDq", ncn), Q2CLAIM[ncn]])
             else:
                 op(eng, lambda e, i=i, s=s, q23=q23: e.tensor_scalar(
                     q23[:, 2 * i:2 * i + 2, :],
                     i2_sl[s].rearrange("q (k f) -> q k f", f=512),
                     s1col[:], None, ALU.mult),
                    reads=[("i2p", s), ("s1col",)],
                    writes=[("VDq", ncn), Q2CLAIM[ncn]])

         out_sl = [SCR32.ap()[:, i * 512:(i + 1) * 512] for i in range(3)]
         OUT_K = [("o32", i) for i in range(3)]
         tmp_sl = [SCR32.ap()[:, 1536:2048], SCR32.ap()[:, 2048:2560]]
         TMP_K = [("tmp", 0), ("tmp", 1)]
         xre_sl = [SCR32.ap()[:, 2560:3072], SCR32.ap()[:, 3072:3584]]
         XRE_K = [("xre", 0), ("xre", 1)]

         at3 = QT.ap().rearrange("q (t h) -> q h t", h=16)
         q13 = Q1T.ap().rearrange("q (k t) -> q k t", k=16)

         def p4_tile(ncn, tt, bA, bF, bAk, bFk, pj, oi, tail=None, xring="sync"):
             op(xring, lambda e, tt=tt, ncn=ncn, pj=pj: e.dma_start(
                 out=xre_sl[pj],
                 in_=x_in[tt * 128:(tt + 1) * 128, ncn * 512:(ncn + 1) * 512]),
                writes=[XRE_K[pj]], q=sXR[pj])
             op("gpsimd", lambda e, ncn=ncn, pj=pj: e.tensor_tensor(
                 xre_sl[pj], xre_sl[pj],
                 btile[:, ncn * 512:(ncn + 1) * 512], ALU.add),
                reads=[XRE_K[pj], ("bt",)], writes=[XRE_K[pj]])

             def agroup(e, bA=bA, tt=tt, ncn=ncn):
                 inst = None
                 w43 = w4buf[ncn].rearrange("q (k n) -> q k n", n=512)
                 for k in range(NK):
                     inst = e.matmul(bA,
                                     at3[:, k, tt * 128:(tt + 1) * 128],
                                     w43[:, k, :],
                                     start=(k == 0), stop=(k == NK - 1))
                 return inst
             op("tensor", agroup,
                reads=[("QTb", 2 * tt), ("QTb", 2 * tt + 1), ("w4", ncn)],
                writes=[bAk])

             def fgroup(e, bF=bF, tt=tt, ncn=ncn):
                 inst = None
                 q23 = q2r[ncn].rearrange("p (k n) -> p k n", n=512)
                 for j2 in range(NK // 2):
                     inst = e.matmul(bF,
                                     q13[:, 2 * j2:2 * j2 + 2, tt * 128:(tt + 1) * 128],
                                     q23[:, 2 * j2:2 * j2 + 2, :],
                                     start=(j2 == 0), stop=(j2 == NK // 2 - 1),
                                     perf_mode=PM.DoubleRow)
                 return inst
             op("tensor", fgroup, reads=[("Q1h", tt // 4), ("VDq", ncn)], writes=[bFk])

             def epi():
                 osl = out_sl[oi]
                 op("vector", lambda e, pj=pj, bF=bF: e.scalar_tensor_tensor(
                     tmp_sl[pj], bF, ccol[:], xre_sl[pj],
                     ALU.mult, ALU.add),
                    reads=[bFk, ("ccol",), XRE_K[pj]], writes=[TMP_K[pj]])
                 op("vector", lambda e, osl=osl, pj=pj, bA=bA: e.tensor_tensor(
                     osl, tmp_sl[pj], bA, ALU.add),
                    reads=[TMP_K[pj], bAk], writes=[OUT_K[oi]])
                 op("scalar", lambda e, osl=osl, tt=tt, ncn=ncn: e.dma_start(
                     out=y_out[tt * 128:(tt + 1) * 128, ncn * 512:(ncn + 1) * 512],
                     in_=osl), reads=[OUT_K[oi]], q=sST[oi])
             if tail is None:
                 epi()
             else:
                 tail.append(epi)

         p4tail = []
         emit_wout(0)
         op("sync", lambda e: e.dma_start(
             out=btile, in_=bout.partition_broadcast(128)),
            writes=[("bt",)], q=sBR)

         stg = ncut == 31 or ncut == 4
         ilv = ncut == 32 or ncut == 4
         for b in range(NB2):
             p3_tensor_front(b)
             p3_expmask(b)
             if stg and 4 <= b < 12:
                 ncn_p = (b - 4) // 4
                 emit_in2_piece(ncn_p, 2 * (b - 4 * ncn_p - 4))
                 emit_in2_piece(ncn_p, 2 * (b - 4 * ncn_p - 4) + 1)
             if stg and 12 <= b < 16:
                 emit_in2_piece(2, 2 * (b - 12))
                 emit_in2_piece(2, 2 * (b - 12) + 1)
             if stg and b == 8:
                 emit_wout(1)
             if b > 1:
                 p3_consume(b - 2)
             if ilv and 10 <= b:
                 t = b - 10
                 while p4tail:
                     p4tail.pop(0)()
                 p4_tile(0, t, PB[2], PB[3], ("PT", 2), ("PT", 3), t % 2, t % 3,
                         tail=p4tail, xring="scalar")
         p3_consume(NB2 - 2)
         p3_consume(NB2 - 1)
         while p4tail:
             p4tail.pop(0)()
         if ilv:
             p4_tile(0, 6, PB[2], PB[3], ("PT", 2), ("PT", 3), 0, 0, xring="scalar")
             p4_tile(0, 7, PB[2], PB[3], ("PT", 2), ("PT", 3), 1, 1, xring="scalar")
         if stg:
             emit_wout(2)

         if ncut != 4:
             raise _Cut
         for ncn in range((1 if ilv else 0), 4):
             for tt in range(NT):
                 j = ncn * NT + tt
                 pj = j % 2
                 p4_tile(ncn, tt, PB[2 * pj], PB[1 + 2 * pj],
                         ("PT", 2 * pj), ("PT", 1 + 2 * pj), pj, j % 3)
                 if ncn == 1 and tt < 8:
                     emit_in2_piece(3, tt)
             if ncn == 1:
                 emit_wout(3)

         for i in range(3):
             emit("sync", lambda e, s_=sST[i].h, v=sST[i].n: e.wait_ge(s_, v))

        try:
            _phases()
        except _Cut:
            op("gpsimd", lambda e: e.dma_start(out=y_out[0:128, :],
               in_=AB.ap().bitcast(F32)[:, 0:2048]), reads=[("ABh", 0)], q=sBR)
            emit("sync", lambda e, s_=sBR.h, v=sBR.n: e.wait_ge(s_, v))

        block = ec(nc.Block())

        @block.sync
        def _(e):
            for f in ops["sync"]:
                f(e)

        @block.gpsimd
        def _(e):
            for f in ops["gpsimd"]:
                f(e)

        @block.vector
        def _(e):
            for f in ops["vector"]:
                f(e)

        @block.scalar
        def _(e):
            for f in ops["scalar"]:
                f(e)

        @block.tensor
        def _(e):
            for f in ops["tensor"]:
                f(e)

    return nc


_NC_CACHE = None


def kernel(L0_attn_x, L0_attn_W_qkv, L0_attn_W_out, L0_attn_b_out,
           L0_ffn_input2, L0_ffn_scale, L0_ffn_scale1):
    global _NC_CACHE
    if _NC_CACHE is None:
        _NC_CACHE = build_program()
    nc = _NC_CACHE

    x = np.ascontiguousarray(np.asarray(L0_attn_x, dtype=np.float32).reshape(8192, 2048))
    wqkv = np.ascontiguousarray(np.asarray(L0_attn_W_qkv, dtype=np.float32))
    wout = np.ascontiguousarray(np.asarray(L0_attn_W_out, dtype=np.float32))
    bout = np.ascontiguousarray(np.asarray(L0_attn_b_out, dtype=np.float32).reshape(1, 2048))
    in2 = np.ascontiguousarray(np.asarray(L0_ffn_input2, dtype=np.float32))
    sc = np.asarray(L0_ffn_scale, dtype=np.float32).reshape(1, 1)
    sc1 = np.asarray(L0_ffn_scale1, dtype=np.float32).reshape(1, 1)

    id32 = np.eye(128, dtype=np.float32)
    idbf = np.eye(128, dtype=ml_dtypes.bfloat16)
    maskb = np.kron(np.eye(8, dtype=np.float32),
                    np.ones((16, 16), dtype=np.float32)).astype(ml_dtypes.bfloat16)

    in_maps = []
    for c in range(8):
        in_maps.append({
            "x": x[c * T:(c + 1) * T],
            "wqkv": wqkv, "wout": wout, "bout": bout, "in2": in2,
            "sc": sc, "sc1": sc1,
            "id32": id32, "idbf": idbf, "maskb": maskb,
        })

    res = run_bass_kernel_spmd(nc, in_maps, core_ids=list(range(8)))
    y = np.concatenate([r["y"] for r in res.results], axis=0)
    return y.reshape(4, 2048, 2048)


# version 31
# speedup vs baseline: 1.0227x; 1.0090x over previous
import numpy as np
import ml_dtypes
import contextlib
import concourse.bass as bass
import concourse.mybir as mybir
from concourse.bass_utils import run_bass_kernel_spmd

F32 = mybir.dt.float32
BF16 = mybir.dt.bfloat16
F8 = mybir.dt.float8e4
PM = mybir.MatmulPerfMode
AF = mybir.ActivationFunctionType
ALU = mybir.AluOpType
AX = mybir.AxisListType

T = 1024
NT = 8
NK = 16
NH = 16
NG = T // 8
NB = NG // 4
CSC = 1.0 / np.sqrt(128.0)

ENGS = ("sync", "gpsimd", "vector", "scalar", "tensor")


class Sem:
    def __init__(self, h):
        self.h = h
        self.n = 0

    def inc(self, k=1):
        self.n += k
        return self.n


class _Cut(Exception):
    pass


def build_program(ncut=4):
    nc = bass.Bass("TRN2", target_bir_lowering=False)

    x_in = nc.dram_tensor("x", [T, 2048], F32, kind="ExternalInput").ap()
    wqkv = nc.dram_tensor("wqkv", [2048, 6144], F32, kind="ExternalInput").ap()
    wout = nc.dram_tensor("wout", [2048, 2048], F32, kind="ExternalInput").ap()
    bout = nc.dram_tensor("bout", [1, 2048], F32, kind="ExternalInput").ap()
    in2 = nc.dram_tensor("in2", [2048, 2048], F32, kind="ExternalInput").ap()
    sc = nc.dram_tensor("sc", [1, 1], F32, kind="ExternalInput").ap()
    sc1 = nc.dram_tensor("sc1", [1, 1], F32, kind="ExternalInput").ap()
    id32 = nc.dram_tensor("id32", [128, 128], F32, kind="ExternalInput").ap()
    idbf = nc.dram_tensor("idbf", [128, 128], BF16, kind="ExternalInput").ap()
    maskb = nc.dram_tensor("maskb", [128, 128], BF16, kind="ExternalInput").ap()
    y_out = nc.dram_tensor("y", [T, 2048], F32, kind="ExternalOutput").ap()

    ops = {e: [] for e in ENGS}

    def emit(eng, fn):
        ops[eng].append(fn)

    es = contextlib.ExitStack()
    with es:
        ec = es.enter_context
        AB = ec(nc.sbuf_tensor("AB", [128, 16384], BF16))
        QT = ec(nc.sbuf_tensor("QT", [128, 16384], BF16))
        KV1 = ec(nc.sbuf_tensor("KV1", [128, 16384], BF16))
        VD = ec(nc.sbuf_tensor("VD", [128, 16384], BF16))
        Q1T = ec(nc.sbuf_tensor("Q1T", [128, 16384], F8))
        WQ = ec(nc.sbuf_tensor("WQ", [128, 8192], BF16))
        SCR32 = ec(nc.sbuf_tensor("SCR32", [128, 7680], F32))
        SCRBF = ec(nc.sbuf_tensor("SCRBF", [128, 8192], BF16))
        id32t = ec(nc.sbuf_tensor("id32t", [128, 128], F32))
        idbft = ec(nc.sbuf_tensor("idbft", [128, 128], BF16))
        maskt = ec(nc.sbuf_tensor("maskt", [128, 128], BF16))
        epsc = ec(nc.sbuf_tensor("epsc", [128, 1], F32))
        scol = ec(nc.sbuf_tensor("scol", [128, 1], F32))
        s1col = ec(nc.sbuf_tensor("s1col", [128, 1], F32))
        ccol = ec(nc.sbuf_tensor("ccol", [128, 1], F32))
        ctmp = ec(nc.sbuf_tensor("ctmp", [128, 2], F32))
        stat = ec(nc.sbuf_tensor("stat", [128, 32], F32))
        rscf = ec(nc.sbuf_tensor("rscf", [128, 32], F32))

        B0 = ec(nc.psum_tensor("B0", [128, 1024], F32))
        B1 = ec(nc.psum_tensor("B1", [128, 1024], F32))
        X0 = ec(nc.psum_tensor("X0", [128, 512], F32))
        X1 = ec(nc.psum_tensor("X1", [128, 512], F32))
        PVP = ec(nc.psum_tensor("PVP", [128, 1024], BF16))
        PWT = ec(nc.psum_tensor("PWT", [128, 1024], BF16))
        PB = [B0.ap()[:, 0:512], B0.ap()[:, 512:1024], X0.ap()[:], X1.ap()[:]]

        sC = Sem(ec(nc.semaphore("sC")))
        sX = [Sem(ec(nc.semaphore(f"sX{i}"))) for i in range(4)]
        sIN = [Sem(ec(nc.semaphore(f"sIN{i}"))) for i in range(2)]
        sXR = [Sem(ec(nc.semaphore(f"sXR{i}"))) for i in range(2)]
        sST = [Sem(ec(nc.semaphore(f"sST{i}"))) for i in range(3)]
        sWQ = [[Sem(ec(nc.semaphore(f"sWQ{i}{h}"))) for h in range(2)] for i in range(2)]
        sWO = [[Sem(ec(nc.semaphore(f"sWO{i}{h}"))) for h in range(2)] for i in range(2)]
        sBR = Sem(ec(nc.semaphore("sBR")))
        sT = Sem(ec(nc.semaphore("sT")))
        sV = Sem(ec(nc.semaphore("sV")))
        sS = Sem(ec(nc.semaphore("sS")))
        sG = Sem(ec(nc.semaphore("sG")))
        ESEM = {"vector": sV, "scalar": sS, "tensor": sT, "gpsimd": sG}

        last_w = {}
        readers = {}
        seen = {e: {} for e in ENGS}

        def _wait(eng, sem_, val):
            if val <= 0 or seen[eng].get(sem_, 0) >= val:
                return
            seen[eng][sem_] = val
            emit(eng, lambda e, s=sem_.h, v=val: e.wait_ge(s, v))

        def op(eng, fn, reads=(), writes=(), q=None):
            deps = {}
            for k in reads:
                w = last_w.get(k)
                if w:
                    deps[w[0]] = max(deps.get(w[0], 0), w[1])
            for k in writes:
                w = last_w.get(k)
                if w:
                    deps[w[0]] = max(deps.get(w[0], 0), w[1])
                for s_, v in readers.get(k, {}).items():
                    deps[s_] = max(deps.get(s_, 0), v)
            for s_, v in deps.items():
                _wait(eng, s_, v)
            sem_ = q if q is not None else ESEM[eng]
            amt = 16 if q is not None else 1
            emit(eng, lambda e, fn=fn, s=sem_.h, a=amt: fn(e).then_inc(s, a))
            t = sem_.inc(amt)
            tok = (sem_, t)
            for k in reads:
                readers.setdefault(k, {})[sem_] = t
            for k in writes:
                last_w[k] = tok
                readers[k] = {}
            return tok

        VDf = VD.ap().bitcast(F32)
        K1f = KV1.ap().bitcast(F32)
        xt_sl = [VDf[:, 0:2048], VDf[:, 2048:4096],
                 K1f[:, 0:2048], K1f[:, 2048:4096]]
        xn_sl = [VDf[:, 4096:6144], VDf[:, 6144:8192],
                 K1f[:, 4096:6144], K1f[:, 6144:8192]]
        XT_K = [[("xt", p)] for p in range(4)]
        XN_K = [[("xn", p)] for p in range(4)]
        BF_K = [("bf", i) for i in range(8)]

        def _phases():
         op("sync", lambda e: e.dma_start(out=id32t[:], in_=id32), writes=[("id32",)], q=sC)
         op("sync", lambda e: e.dma_start(out=idbft[:], in_=idbf), writes=[("idbf",)], q=sC)
         op("sync", lambda e: e.dma_start(out=maskt[:], in_=maskb), writes=[("maskt",)], q=sC)
         op("sync", lambda e: e.dma_start(out=scol[:], in_=sc.partition_broadcast(128)),
            writes=[("scol",)], q=sC)
         op("sync", lambda e: e.dma_start(out=s1col[:], in_=sc1.partition_broadcast(128)),
            writes=[("s1col",)], q=sC)
         for _k in (("id32",), ("idbf",), ("maskt",), ("scol",), ("s1col",)):
             last_w[_k] = (sC, sC.n)
         op("vector", lambda e: e.memset(epsc[:], 1e-5), writes=[("epsc",)])
         op("vector", lambda e: e.reciprocal(ctmp[:, 0:1], scol[:]),
            reads=[("scol",)], writes=[("ctmp",)])
         op("vector", lambda e: e.reciprocal(ctmp[:, 1:2], s1col[:]),
            reads=[("s1col",), ("ctmp",)], writes=[("ctmp",)])
         op("vector", lambda e: e.tensor_tensor(ccol[:], ctmp[:, 0:1], ctmp[:, 1:2], ALU.mult),
            reads=[("ctmp",)], writes=[("ccol",)])

         wq_slot = [WQ.ap()[:, 0:4096], WQ.ap()[:, 4096:8192]]

         def emit_wq(p):
             s = p % 2
             g = p % 24
             eng = "gpsimd"
             for h2 in range(2):
                 op(eng, lambda e, g=g, s=s, h2=h2: e.dma_start(
                     out=wq_slot[s].rearrange("q (k f) -> q k f", f=256)
                         [:, 8 * h2:8 * h2 + 8, :],
                     in_=wqkv.rearrange("(k q) (g f) -> q k g f", q=128, f=256)
                         [:, 8 * h2:8 * h2 + 8, g, :]),
                    writes=[("wq", s, h2)], q=sWQ[s][h2])

         emit_wq(0)
         emit_wq(1)

         btile = SCR32.ap()[:, 3584:5632]
         op("scalar", lambda e: e.dma_start(
             out=btile, in_=bout.partition_broadcast(128)),
            writes=[("bt",)], q=sBR)

         def p1_load_stats(tt):
             p = tt % 4
             op(("sync", "scalar")[tt % 2],
                lambda e, p=p, tt=tt: e.dma_start(
                    out=xt_sl[p], in_=x_in[tt * 128:(tt + 1) * 128, :]),
                writes=XT_K[p], q=sX[p])

             st = stat.ap()[:, 8 * p:8 * p + 8]
             msum, m_, m2, varep, sq_, rstd, ssq = [st[:, i:i + 1] for i in range(7)]
             SK = [("stat", p)]

             op("vector", lambda e, p=p, msum=msum: e.reduce_sum(msum, xt_sl[p], AX.X),
                reads=XT_K[p], writes=SK)
             op("scalar", lambda e, p=p, ssq=ssq: e.activation(
                 xn_sl[p], xt_sl[p], AF.Square, accum_out=ssq),
                reads=XT_K[p], writes=XN_K[p] + SK)
             op("vector", lambda e, msum=msum, m_=m_: e.tensor_scalar(
                 m_, msum, 1.0 / 2048.0, None, ALU.mult), reads=SK, writes=SK)
             op("vector", lambda e, msum=msum, m2=m2: e.tensor_scalar(
                 m2, msum, msum, 1.0 / 2048.0 ** 2, ALU.mult, ALU.mult),
                reads=SK, writes=SK)
             op("vector", lambda e, ssq=ssq, m2=m2, varep=varep: e.scalar_tensor_tensor(
                 varep, ssq, 1.0 / 2048.0, m2, ALU.mult, ALU.subtract),
                reads=SK, writes=SK)
             op("scalar", lambda e, varep=varep, sq_=sq_: e.activation(
                 sq_, varep, AF.Sqrt, bias=epsc[:]), reads=SK + [("epsc",)], writes=SK)
             op("vector", lambda e, sq_=sq_, rstd=rstd: e.reciprocal(rstd, sq_),
                reads=SK, writes=SK)

         def p1_xn_transpose(tt):
             p = tt % 4
             st = stat.ap()[:, 8 * p:8 * p + 8]
             m_, rstd = st[:, 1:2], st[:, 5:6]
             SK = [("stat", p)]
             op("vector", lambda e, p=p, m_=m_, rstd=rstd: e.tensor_scalar(
                 xn_sl[p], xt_sl[p], m_, rstd, ALU.subtract, ALU.mult),
                reads=XT_K[p] + SK, writes=XN_K[p])

             half = tt // 4
             for j in range(4):
                 bank = PB[(tt * 4 + j) % 4]
                 bk = ("PT", (tt * 4 + j) % 4)

                 def trgroup(e, bank=bank, p=p, j=j):
                     inst = None
                     for i in range(4):
                         k = 4 * j + i
                         inst = e.transpose(bank[:, i * 128:(i + 1) * 128],
                                            xn_sl[p][:, k * 128:(k + 1) * 128], id32t[:])
                     return inst
                 op("tensor", trgroup, reads=XN_K[p] + [("id32",)], writes=[bk])

                 dst_xnT = AB.ap().rearrange("q (k t) -> q k t", k=16)[
                     :, 4 * j:4 * j + 4, tt * 128:(tt + 1) * 128]
                 src3 = bank.rearrange("q (k t) -> q k t", k=4)
                 dst_q1 = Q1T.ap().rearrange("q (k t) -> q k t", k=16)[
                     :, 4 * j:4 * j + 4, tt * 128:(tt + 1) * 128]
                 if tt % 2 == 0:
                     op("scalar", lambda e, dst=dst_xnT, src=src3: e.copy(dst, src),
                        reads=[bk], writes=[("ABh", half), ("rr",) + bk])
                     op("vector", lambda e, dst=dst_q1, bank=bank: e.tensor_scalar(
                         dst, bank.rearrange("q (k t) -> q k t", k=4), scol[:],
                         None, ALU.mult),
                        reads=[bk, ("scol",), ("rr",) + bk], writes=[("Q1h", half)])
                 else:
                     op("vector", lambda e, dst=dst_xnT, src=src3: e.tensor_copy(dst, src),
                        reads=[bk], writes=[("ABh", half), ("rr",) + bk])
                     op("scalar", lambda e, dst=dst_q1, bank=bank: e.activation(
                         dst, bank.rearrange("q (k t) -> q k t", k=4),
                         AF.Copy, scale=scol[:]),
                        reads=[bk, ("scol",), ("rr",) + bk], writes=[("Q1h", half)])

         p1_load_stats(0)
         p1_load_stats(1)
         for tt in range(4):
             p1_xn_transpose(tt)
             if tt + 2 < 8:
                 p1_load_stats(tt + 2)

         if ncut < 2:
             raise _Cut
         xnT3 = AB.ap().rearrange("q (k t) -> q k t", k=16)
         kv_view = {
             "k": KV1.ap().rearrange("q (g s H) -> q g s H", s=8, H=16),
             "v": VD.ap().rearrange("q (g s H) -> q g s H", s=8, H=16),
         }
         KV_KEYS = {("k", 0): ("K1A",), ("k", 1): ("K1B",),
                    ("v", 0): ("VDA",), ("v", 1): ("VDB",)}
         QTB = [[("QTb", b) for b in range(8 * hf, 8 * hf + 8)] for hf in range(2)]

         def p2_mm(ft, half):
             s = (ft // 2) % 2
             ftl = ft % 2
             bank = PB[(2 * ft + half) % 4]
             bk = ("PT", (2 * ft + half) % 4)

             def mmgroup(e, bank=bank, s=s, ftl=ftl, half=half):
                 inst = None
                 w3 = wq_slot[s].rearrange("q (k f) -> q k f", f=256)
                 for j in range(NK):
                     inst = e.matmul(
                         bank,
                         w3[:, j, ftl * 128:(ftl + 1) * 128],
                         xnT3[:, j, half * 512:(half + 1) * 512],
                         start=(j == 0), stop=(j == NK - 1))
                 return inst
             op("tensor", mmgroup,
                reads=[("wq", s, 0), ("wq", s, 1), ("ABh", half)], writes=[bk])

             if ft < 16:
                 h = ft
                 dstq = QT.ap().rearrange("q (g s H) -> q g s H", s=8, H=16)[
                     :, half * 64:(half + 1) * 64, :, h]
                 op("vector", lambda e, dstq=dstq, bank=bank: e.tensor_copy(
                     dstq, bank.rearrange("q (g s) -> q g s", s=8)),
                    reads=[bk], writes=QTB[half])
             else:
                 H = (ft - 16) % 16
                 which = "k" if ft < 32 else "v"
                 eng = "scalar"
                 dst = kv_view[which][:, half * 64:(half + 1) * 64, :, H]
                 wr = [KV_KEYS[(which, half)]]
                 if which == "v":
                     wr += [("VDq4", 2 * half), ("VDq4", 2 * half + 1)]
                 if H == 0:
                     if which == "v" and half == 0:
                         wr += [("xt", 0), ("xt", 1)]
                     elif which == "v" and half == 1:
                         wr += [("xn", 0), ("xn", 1)]
                     elif which == "k" and half == 0:
                         wr += [("xt", 2), ("xt", 3)]
                     else:
                         wr += [("xn", 2), ("xn", 3)]
                 op(eng, lambda e, dst=dst, bank=bank: e.copy(
                     dst, bank.rearrange("q (g s) -> q g s", s=8)),
                    reads=[bk], writes=wr)

         for ft in range(4):
             p1_xn_transpose(ft + 4)
             if ft + 6 < 8:
                 p1_load_stats(ft + 6)
             p2_mm(ft, 0)
             if ft % 2 == 1:
                 emit_wq(ft // 2 + 2)
         for ft in range(4, 48):
             p2_mm(ft, 0)
             p2_mm(ft, 1)
             if ft % 2 == 1 and ft // 2 + 2 < 24:
                 emit_wq(ft // 2 + 2)
         emit_wq(24)
         emit_wq(25)
         for ft in range(4):
             s = (ft // 2) % 2
             ftl = ft % 2
             bank = PB[(2 * ft + 1) % 4]
             bk = ("PT", (2 * ft + 1) % 4)

             def mmgroup(e, bank=bank, s=s, ftl=ftl):
                 inst = None
                 w3 = wq_slot[s].rearrange("q (k f) -> q k f", f=256)
                 for j in range(NK):
                     inst = e.matmul(
                         bank,
                         w3[:, j, ftl * 128:(ftl + 1) * 128],
                         xnT3[:, j, 512:1024],
                         start=(j == 0), stop=(j == NK - 1))
                 return inst
             op("tensor", mmgroup,
                reads=[("wq", s, 0), ("wq", s, 1), ("ABh", 1)], writes=[bk])
             dstq = QT.ap().rearrange("q (g s H) -> q g s H", s=8, H=16)[
                 :, 64:128, :, ft]
             op("vector", lambda e, dstq=dstq, bank=bank: e.tensor_copy(
                 dstq, bank.rearrange("q (g s) -> q g s", s=8)),
                reads=[bk], writes=QTB[1])

         if ncut < 3:
             raise _Cut
         NB2 = 16
         wm_sl = [SCRBF.ap()[:, i * 1024:(i + 1) * 1024] for i in range(3)]
         vp_sl = [SCRBF.ap()[:, 3072 + i * 1024:3072 + (i + 1) * 1024] for i in range(3)]
         wmn_sl = SCRBF.ap()[:, 6144:7168]
         wmt_sl = SCRBF.ap()[:, 7168:8192]
         WM_K = [("bf", i) for i in range(3)]
         VP_K = [("bf", 3 + i) for i in range(3)]
         WMN_K = ("wmn",)
         WMT_K = ("wmt",)
         mask_b = maskt.ap().unsqueeze(1).broadcast_to((128, 8, 128))
         SCB = [B0, B1]
         SCKs = [[("PT", 0), ("PT", 1)], [("B1",)]]

         def p3_tensor_front(b):
             pb = b % 2

             def vpgroup(e, b=b):
                 inst = None
                 for i in range(8):
                     g = 8 * b + i
                     inst = e.transpose(PVP.ap()[:, i * 128:(i + 1) * 128],
                                        VD.ap()[:, g * 128:(g + 1) * 128], idbft[:])
                 return inst
             vk = [("VDq4", b // 4), ("idbf",)]
             op("tensor", vpgroup, reads=vk, writes=[("PV",)])

             def scgroup(e, b=b, pb=pb):
                 inst = None
                 for i in range(8):
                     g = 8 * b + i
                     inst = e.matmul(SCB[pb].ap()[:, i * 128:(i + 1) * 128],
                                     QT.ap()[:, g * 128:(g + 1) * 128],
                                     KV1.ap()[:, g * 128:(g + 1) * 128],
                                     start=True, stop=True)
                 return inst
             op("tensor", scgroup,
                reads=[("K1A",) if b < 8 else ("K1B",), ("QTb", b)], writes=SCKs[pb])

         def p3_expmask(b):
             pb = b % 2
             p3 = b % 3
             if pb == 0:
                 op("vector", lambda e, p3=p3: e.tensor_copy(vp_sl[p3], PVP.ap()[:]),
                    reads=[("PV",)], writes=[VP_K[p3]])
             else:
                 op("scalar", lambda e, p3=p3: e.copy(vp_sl[p3], PVP.ap()[:]),
                    reads=[("PV",)], writes=[VP_K[p3]])
             op("scalar", lambda e, pb=pb, p3=p3: e.activation(
                 wm_sl[p3], SCB[pb].ap()[:], AF.Exp, scale=CSC),
                reads=SCKs[pb], writes=[WM_K[p3]])
             op("gpsimd", lambda e, p3=p3: e.tensor_tensor(
                 wm_sl[p3].rearrange("q (i m) -> q i m", i=8),
                 wm_sl[p3].rearrange("q (i m) -> q i m", i=8),
                 mask_b, ALU.mult),
                reads=[WM_K[p3], ("maskt",)], writes=[WM_K[p3]])

         def p3_consume(b):
             pb = b % 2
             p3 = b % 3
             dn = rscf.ap()[:, 16 * pb:16 * pb + 8]
             rq = rscf.ap()[:, 16 * pb + 8:16 * pb + 16]

             op("vector", lambda e, p3=p3, dn=dn: e.reduce_sum(
                 dn, wm_sl[p3].rearrange("q (i m) -> q i m", i=8), AX.X),
                reads=[WM_K[p3]], writes=[("dn", pb)])
             op("vector", lambda e, dn=dn, rq=rq: e.reciprocal(rq, dn),
                reads=[("dn", pb)], writes=[("rq", pb)])
             op("vector", lambda e, p3=p3, rq=rq: e.tensor_tensor(
                 wmn_sl.rearrange("q (i m) -> q i m", i=8),
                 wm_sl[p3].rearrange("q (i m) -> q i m", i=8),
                 rq.unsqueeze(2).broadcast_to((128, 8, 128)), ALU.mult),
                reads=[WM_K[p3], ("rq", pb)], writes=[WMN_K])

             def wtgroup(e):
                 inst = None
                 for i in range(8):
                     inst = e.transpose(PWT.ap()[:, i * 128:(i + 1) * 128],
                                        wmn_sl[:, i * 128:(i + 1) * 128], idbft[:])
                 return inst
             op("tensor", wtgroup, reads=[WMN_K, ("idbf",)], writes=[("PW",)])

             if pb == 0:
                 op("vector", lambda e: e.tensor_copy(wmt_sl, PWT.ap()[:]),
                    reads=[("PW",)], writes=[WMT_K])
             else:
                 op("scalar", lambda e: e.copy(wmt_sl, PWT.ap()[:]),
                    reads=[("PW",)], writes=[WMT_K])

             def atgroup(e, pb=pb, p3=p3):
                 inst = None
                 for i in range(8):
                     inst = e.matmul(SCB[pb].ap()[:, i * 128:(i + 1) * 128],
                                     vp_sl[p3][:, i * 128:(i + 1) * 128],
                                     wmt_sl[:, i * 128:(i + 1) * 128],
                                     start=True, stop=True)
                 return inst
             op("tensor", atgroup, reads=[VP_K[p3], WMT_K], writes=SCKs[pb])

             if pb == 0:
                 op("scalar", lambda e, b=b, pb=pb: e.copy(
                     QT.ap()[:, 1024 * b:1024 * b + 1024], SCB[pb].ap()[:]),
                    reads=SCKs[pb], writes=[("QTb", b)])
             else:
                 op("vector", lambda e, b=b, pb=pb: e.tensor_copy(
                     QT.ap()[:, 1024 * b:1024 * b + 1024], SCB[pb].ap()[:]),
                    reads=SCKs[pb], writes=[("QTb", b)])

         w4buf = [WQ.ap()[:, 0:8192], KV1.ap()[:, 0:8192],
                  KV1.ap()[:, 8192:16384], WQ.ap()[:, 0:8192]]
         W4CLAIM = [[("wq", 0, 0), ("wq", 0, 1), ("wq", 1, 0), ("wq", 1, 1)],
                    [("K1A",)], [("K1B",)], [("w4", 0)]]
         q2r = [VD.ap()[:, 4096 * i:4096 * (i + 1)].bitcast(F8)[:, 0:8192]
                for i in range(4)]

         def emit_wout(ncn):
             op("gpsimd", lambda e, ncn=ncn: e.dma_start(
                 out=w4buf[ncn].rearrange("q (k f) -> q k f", f=512),
                 in_=wout.rearrange("(k q) (n f) -> q k n f", q=128, f=512)
                     [:, :, ncn, :]),
                writes=[("w4", ncn)] + W4CLAIM[ncn], q=sWO[ncn % 2][ncn // 2])

         i2_sl = [SCR32.ap()[:, 5632:6656], SCR32.ap()[:, 6656:7680]]

         def emit_in2_piece(ncn, i):
             s = i % 2
             op("sync", lambda e, ncn=ncn, i=i, s=s: e.dma_start(
                 out=i2_sl[s].rearrange("q (k f) -> q k f", f=512),
                 in_=in2.rearrange("(k q) (n f) -> q k n f", q=128, f=512)
                     [:, 2 * i:2 * i + 2, ncn, :]),
                writes=[("i2p", s)], q=sIN[s])
             q23 = q2r[ncn].rearrange("q (k f) -> q k f", f=512)
             eng = "scalar" if i % 2 == 0 else "vector"
             if eng == "scalar":
                 op(eng, lambda e, i=i, s=s, q23=q23: e.activation(
                     q23[:, 2 * i:2 * i + 2, :],
                     i2_sl[s].rearrange("q (k f) -> q k f", f=512),
                     AF.Copy, scale=s1col[:]),
                    reads=[("i2p", s), ("s1col",)],
                    writes=[("VDq", ncn), ("VDq4", ncn)])
             else:
                 op(eng, lambda e, i=i, s=s, q23=q23: e.tensor_scalar(
                     q23[:, 2 * i:2 * i + 2, :],
                     i2_sl[s].rearrange("q (k f) -> q k f", f=512),
                     s1col[:], None, ALU.mult),
                    reads=[("i2p", s), ("s1col",)],
                    writes=[("VDq", ncn), ("VDq4", ncn)])

         out_sl = [SCR32.ap()[:, i * 512:(i + 1) * 512] for i in range(3)]
         OUT_K = [("o32", i) for i in range(3)]
         tmp_sl = [SCR32.ap()[:, 1536:2048], SCR32.ap()[:, 2048:2560]]
         TMP_K = [("tmp", 0), ("tmp", 1)]
         xre_sl = [SCR32.ap()[:, 2560:3072], SCR32.ap()[:, 3072:3584]]
         XRE_K = [("xre", 0), ("xre", 1)]

         at3 = QT.ap().rearrange("q (t h) -> q h t", h=16)
         q13 = Q1T.ap().rearrange("q (k t) -> q k t", k=16)

         def p4_tile(ncn, tt, bA, bF, bAk, bFk, pj, oi, tail=None, xring="sync"):
             op(xring, lambda e, tt=tt, ncn=ncn, pj=pj: e.dma_start(
                 out=xre_sl[pj],
                 in_=x_in[tt * 128:(tt + 1) * 128, ncn * 512:(ncn + 1) * 512]),
                writes=[XRE_K[pj]], q=sXR[pj])
             op("gpsimd", lambda e, ncn=ncn, pj=pj: e.tensor_tensor(
                 xre_sl[pj], xre_sl[pj],
                 btile[:, ncn * 512:(ncn + 1) * 512], ALU.add),
                reads=[XRE_K[pj], ("bt",)], writes=[XRE_K[pj]])

             def agroup(e, bA=bA, tt=tt, ncn=ncn):
                 inst = None
                 w43 = w4buf[ncn].rearrange("q (k n) -> q k n", n=512)
                 for k in range(NK):
                     inst = e.matmul(bA,
                                     at3[:, k, tt * 128:(tt + 1) * 128],
                                     w43[:, k, :],
                                     start=(k == 0), stop=(k == NK - 1))
                 return inst
             op("tensor", agroup,
                reads=[("QTb", 2 * tt), ("QTb", 2 * tt + 1), ("w4", ncn)],
                writes=[bAk])

             def fgroup(e, bF=bF, tt=tt, ncn=ncn):
                 inst = None
                 q23 = q2r[ncn].rearrange("p (k n) -> p k n", n=512)
                 for j2 in range(NK // 2):
                     inst = e.matmul(bF,
                                     q13[:, 2 * j2:2 * j2 + 2, tt * 128:(tt + 1) * 128],
                                     q23[:, 2 * j2:2 * j2 + 2, :],
                                     start=(j2 == 0), stop=(j2 == NK // 2 - 1),
                                     perf_mode=PM.DoubleRow)
                 return inst
             op("tensor", fgroup, reads=[("Q1h", tt // 4), ("VDq", ncn)], writes=[bFk])

             def epi():
                 osl = out_sl[oi]
                 op("vector", lambda e, pj=pj, bF=bF: e.scalar_tensor_tensor(
                     tmp_sl[pj], bF, ccol[:], xre_sl[pj],
                     ALU.mult, ALU.add),
                    reads=[bFk, ("ccol",), XRE_K[pj]], writes=[TMP_K[pj]])
                 op("vector", lambda e, osl=osl, pj=pj, bA=bA: e.tensor_tensor(
                     osl, tmp_sl[pj], bA, ALU.add),
                    reads=[TMP_K[pj], bAk], writes=[OUT_K[oi]])
                 op("scalar", lambda e, osl=osl, tt=tt, ncn=ncn: e.dma_start(
                     out=y_out[tt * 128:(tt + 1) * 128, ncn * 512:(ncn + 1) * 512],
                     in_=osl), reads=[OUT_K[oi]], q=sST[oi])
             if tail is None:
                 epi()
             else:
                 tail.append(epi)

         p4tail = []
         emit_wout(0)
         op("sync", lambda e: e.dma_start(
             out=btile, in_=bout.partition_broadcast(128)),
            writes=[("bt",)], q=sBR)

         stg = ncut == 31 or ncut == 4
         ilv = ncut == 32 or ncut == 4
         for b in range(NB2):
             if b > 1:
                 p3_consume(b - 2)
             p3_tensor_front(b)
             p3_expmask(b)
             if stg and 4 <= b < 12:
                 ncn_p = (b - 4) // 4
                 emit_in2_piece(ncn_p, 2 * (b - 4 * ncn_p - 4))
                 emit_in2_piece(ncn_p, 2 * (b - 4 * ncn_p - 4) + 1)
             if stg and 12 <= b < 16:
                 emit_in2_piece(2, 2 * (b - 12))
                 emit_in2_piece(2, 2 * (b - 12) + 1)
             if stg and b == 8:
                 emit_wout(1)
             if ilv and 10 <= b:
                 t = b - 10
                 while p4tail:
                     p4tail.pop(0)()
                 p4_tile(0, t, PB[2], PB[3], ("PT", 2), ("PT", 3), t % 2, t % 3,
                         tail=p4tail, xring="scalar")
         p3_consume(NB2 - 2)
         p3_consume(NB2 - 1)
         while p4tail:
             p4tail.pop(0)()
         if ilv:
             p4_tile(0, 6, PB[2], PB[3], ("PT", 2), ("PT", 3), 0, 0, xring="scalar")
             p4_tile(0, 7, PB[2], PB[3], ("PT", 2), ("PT", 3), 1, 1, xring="scalar")
         if stg:
             emit_wout(2)

         if ncut != 4:
             raise _Cut
         for ncn in range((1 if ilv else 0), 4):
             for tt in range(NT):
                 j = ncn * NT + tt
                 pj = j % 2
                 p4_tile(ncn, tt, PB[2 * pj], PB[1 + 2 * pj],
                         ("PT", 2 * pj), ("PT", 1 + 2 * pj), pj, j % 3)
                 if ncn == 1 and tt < 8:
                     emit_in2_piece(3, tt)
             if ncn == 1:
                 emit_wout(3)

         for i in range(3):
             emit("sync", lambda e, s_=sST[i].h, v=sST[i].n: e.wait_ge(s_, v))

        try:
            _phases()
        except _Cut:
            op("gpsimd", lambda e: e.dma_start(out=y_out[0:128, :],
               in_=AB.ap().bitcast(F32)[:, 0:2048]), reads=[("ABh", 0)], q=sBR)
            emit("sync", lambda e, s_=sBR.h, v=sBR.n: e.wait_ge(s_, v))

        block = ec(nc.Block())

        @block.sync
        def _(e):
            for f in ops["sync"]:
                f(e)

        @block.gpsimd
        def _(e):
            for f in ops["gpsimd"]:
                f(e)

        @block.vector
        def _(e):
            for f in ops["vector"]:
                f(e)

        @block.scalar
        def _(e):
            for f in ops["scalar"]:
                f(e)

        @block.tensor
        def _(e):
            for f in ops["tensor"]:
                f(e)

    return nc


_NC_CACHE = None


def kernel(L0_attn_x, L0_attn_W_qkv, L0_attn_W_out, L0_attn_b_out,
           L0_ffn_input2, L0_ffn_scale, L0_ffn_scale1):
    global _NC_CACHE
    if _NC_CACHE is None:
        _NC_CACHE = build_program()
    nc = _NC_CACHE

    x = np.ascontiguousarray(np.asarray(L0_attn_x, dtype=np.float32).reshape(8192, 2048))
    wqkv = np.ascontiguousarray(np.asarray(L0_attn_W_qkv, dtype=np.float32))
    wout = np.ascontiguousarray(np.asarray(L0_attn_W_out, dtype=np.float32))
    bout = np.ascontiguousarray(np.asarray(L0_attn_b_out, dtype=np.float32).reshape(1, 2048))
    in2 = np.ascontiguousarray(np.asarray(L0_ffn_input2, dtype=np.float32))
    sc = np.asarray(L0_ffn_scale, dtype=np.float32).reshape(1, 1)
    sc1 = np.asarray(L0_ffn_scale1, dtype=np.float32).reshape(1, 1)

    id32 = np.eye(128, dtype=np.float32)
    idbf = np.eye(128, dtype=ml_dtypes.bfloat16)
    maskb = np.kron(np.eye(8, dtype=np.float32),
                    np.ones((16, 16), dtype=np.float32)).astype(ml_dtypes.bfloat16)

    in_maps = []
    for c in range(8):
        in_maps.append({
            "x": x[c * T:(c + 1) * T],
            "wqkv": wqkv, "wout": wout, "bout": bout, "in2": in2,
            "sc": sc, "sc1": sc1,
            "id32": id32, "idbf": idbf, "maskb": maskb,
        })

    res = run_bass_kernel_spmd(nc, in_maps, core_ids=list(range(8)))
    y = np.concatenate([r["y"] for r in res.results], axis=0)
    return y.reshape(4, 2048, 2048)


# version 32
# speedup vs baseline: 1.0634x; 1.0397x over previous
import numpy as np
import ml_dtypes
import contextlib
import concourse.bass as bass
import concourse.mybir as mybir
from concourse.bass_utils import run_bass_kernel_spmd

F32 = mybir.dt.float32
BF16 = mybir.dt.bfloat16
F8 = mybir.dt.float8e4
PM = mybir.MatmulPerfMode
AF = mybir.ActivationFunctionType
ALU = mybir.AluOpType
AX = mybir.AxisListType

T = 1024
NT = 8
NK = 16
NH = 16
NG = T // 8
NB = NG // 4
CSC = 1.0 / np.sqrt(128.0)

ENGS = ("sync", "gpsimd", "vector", "scalar", "tensor")


class Sem:
    def __init__(self, h):
        self.h = h
        self.n = 0

    def inc(self, k=1):
        self.n += k
        return self.n


class _Cut(Exception):
    pass


def build_program(ncut=4):
    nc = bass.Bass("TRN2", target_bir_lowering=False)

    x_in = nc.dram_tensor("x", [T, 2048], F32, kind="ExternalInput").ap()
    wqkv = nc.dram_tensor("wqkv", [2048, 6144], F32, kind="ExternalInput").ap()
    wout = nc.dram_tensor("wout", [2048, 2048], F32, kind="ExternalInput").ap()
    bout = nc.dram_tensor("bout", [1, 2048], F32, kind="ExternalInput").ap()
    in2 = nc.dram_tensor("in2", [2048, 2048], F32, kind="ExternalInput").ap()
    sc = nc.dram_tensor("sc", [1, 1], F32, kind="ExternalInput").ap()
    sc1 = nc.dram_tensor("sc1", [1, 1], F32, kind="ExternalInput").ap()
    id32 = nc.dram_tensor("id32", [128, 128], F32, kind="ExternalInput").ap()
    idbf = nc.dram_tensor("idbf", [128, 128], BF16, kind="ExternalInput").ap()
    maskb = nc.dram_tensor("maskb", [128, 128], BF16, kind="ExternalInput").ap()
    y_out = nc.dram_tensor("y", [T, 2048], F32, kind="ExternalOutput").ap()

    ops = {e: [] for e in ENGS}

    def emit(eng, fn):
        ops[eng].append(fn)

    es = contextlib.ExitStack()
    with es:
        ec = es.enter_context
        AB = ec(nc.sbuf_tensor("AB", [128, 16384], BF16))
        QT = ec(nc.sbuf_tensor("QT", [128, 16384], BF16))
        KV1 = ec(nc.sbuf_tensor("KV1", [128, 16384], BF16))
        VD = ec(nc.sbuf_tensor("VD", [128, 16384], BF16))
        Q1T = ec(nc.sbuf_tensor("Q1T", [128, 16384], F8))
        WQ = ec(nc.sbuf_tensor("WQ", [128, 8192], BF16))
        SCR32 = ec(nc.sbuf_tensor("SCR32", [128, 7680], F32))
        SCRBF = ec(nc.sbuf_tensor("SCRBF", [128, 8192], BF16))
        id32t = ec(nc.sbuf_tensor("id32t", [128, 128], F32))
        idbft = ec(nc.sbuf_tensor("idbft", [128, 128], BF16))
        maskt = ec(nc.sbuf_tensor("maskt", [128, 128], BF16))
        epsc = ec(nc.sbuf_tensor("epsc", [128, 1], F32))
        scol = ec(nc.sbuf_tensor("scol", [128, 1], F32))
        s1col = ec(nc.sbuf_tensor("s1col", [128, 1], F32))
        ccol = ec(nc.sbuf_tensor("ccol", [128, 1], F32))
        ctmp = ec(nc.sbuf_tensor("ctmp", [128, 2], F32))
        stat = ec(nc.sbuf_tensor("stat", [128, 32], F32))
        rscf = ec(nc.sbuf_tensor("rscf", [128, 32], F32))

        B0 = ec(nc.psum_tensor("B0", [128, 1024], F32))
        B1 = ec(nc.psum_tensor("B1", [128, 1024], F32))
        X0 = ec(nc.psum_tensor("X0", [128, 512], F32))
        X1 = ec(nc.psum_tensor("X1", [128, 512], F32))
        PVP = ec(nc.psum_tensor("PVP", [128, 1024], BF16))
        PWT = ec(nc.psum_tensor("PWT", [128, 1024], BF16))
        PB = [B0.ap()[:, 0:512], B0.ap()[:, 512:1024], X0.ap()[:], X1.ap()[:]]

        sC = Sem(ec(nc.semaphore("sC")))
        sX = [Sem(ec(nc.semaphore(f"sX{i}"))) for i in range(4)]
        sIN = [Sem(ec(nc.semaphore(f"sIN{i}"))) for i in range(2)]
        sXR = [Sem(ec(nc.semaphore(f"sXR{i}"))) for i in range(2)]
        sST = [Sem(ec(nc.semaphore(f"sST{i}"))) for i in range(3)]
        sWQ = [[Sem(ec(nc.semaphore(f"sWQ{i}{h}"))) for h in range(2)] for i in range(2)]
        sWO = [[Sem(ec(nc.semaphore(f"sWO{i}{h}"))) for h in range(2)] for i in range(2)]
        sBR = Sem(ec(nc.semaphore("sBR")))
        sT = Sem(ec(nc.semaphore("sT")))
        sV = Sem(ec(nc.semaphore("sV")))
        sS = Sem(ec(nc.semaphore("sS")))
        sG = Sem(ec(nc.semaphore("sG")))
        ESEM = {"vector": sV, "scalar": sS, "tensor": sT, "gpsimd": sG}

        last_w = {}
        readers = {}
        seen = {e: {} for e in ENGS}

        def _wait(eng, sem_, val):
            if val <= 0 or seen[eng].get(sem_, 0) >= val:
                return
            seen[eng][sem_] = val
            emit(eng, lambda e, s=sem_.h, v=val: e.wait_ge(s, v))

        def op(eng, fn, reads=(), writes=(), q=None):
            deps = {}
            for k in reads:
                w = last_w.get(k)
                if w:
                    deps[w[0]] = max(deps.get(w[0], 0), w[1])
            for k in writes:
                w = last_w.get(k)
                if w:
                    deps[w[0]] = max(deps.get(w[0], 0), w[1])
                for s_, v in readers.get(k, {}).items():
                    deps[s_] = max(deps.get(s_, 0), v)
            for s_, v in deps.items():
                _wait(eng, s_, v)
            sem_ = q if q is not None else ESEM[eng]
            amt = 16 if q is not None else 1
            emit(eng, lambda e, fn=fn, s=sem_.h, a=amt: fn(e).then_inc(s, a))
            t = sem_.inc(amt)
            tok = (sem_, t)
            for k in reads:
                readers.setdefault(k, {})[sem_] = t
            for k in writes:
                last_w[k] = tok
                readers[k] = {}
            return tok

        VDf = VD.ap().bitcast(F32)
        K1f = KV1.ap().bitcast(F32)
        xt_sl = [VDf[:, 0:2048], VDf[:, 2048:4096],
                 K1f[:, 0:2048], K1f[:, 2048:4096]]
        xn_sl = [VDf[:, 4096:6144], VDf[:, 6144:8192],
                 K1f[:, 4096:6144], K1f[:, 6144:8192]]
        XT_K = [[("xt", p)] for p in range(4)]
        XN_K = [[("xn", p)] for p in range(4)]
        BF_K = [("bf", i) for i in range(8)]

        def _phases():
         op("sync", lambda e: e.dma_start(out=id32t[:], in_=id32), writes=[("id32",)], q=sC)
         op("sync", lambda e: e.dma_start(out=idbft[:], in_=idbf), writes=[("idbf",)], q=sC)
         op("sync", lambda e: e.dma_start(out=maskt[:], in_=maskb), writes=[("maskt",)], q=sC)
         op("sync", lambda e: e.dma_start(out=scol[:], in_=sc.partition_broadcast(128)),
            writes=[("scol",)], q=sC)
         op("sync", lambda e: e.dma_start(out=s1col[:], in_=sc1.partition_broadcast(128)),
            writes=[("s1col",)], q=sC)
         for _k in (("id32",), ("idbf",), ("maskt",), ("scol",), ("s1col",)):
             last_w[_k] = (sC, sC.n)
         op("vector", lambda e: e.memset(epsc[:], 1e-5), writes=[("epsc",)])
         op("vector", lambda e: e.reciprocal(ctmp[:, 0:1], scol[:]),
            reads=[("scol",)], writes=[("ctmp",)])
         op("vector", lambda e: e.reciprocal(ctmp[:, 1:2], s1col[:]),
            reads=[("s1col",), ("ctmp",)], writes=[("ctmp",)])
         op("vector", lambda e: e.tensor_tensor(ccol[:], ctmp[:, 0:1], ctmp[:, 1:2], ALU.mult),
            reads=[("ctmp",)], writes=[("ccol",)])

         wq_slot = [WQ.ap()[:, 0:4096], WQ.ap()[:, 4096:8192]]

         def emit_wq(p):
             s = p % 2
             g = p % 24
             eng = "gpsimd"
             for h2 in range(2):
                 op(eng, lambda e, g=g, s=s, h2=h2: e.dma_start(
                     out=wq_slot[s].rearrange("q (k f) -> q k f", f=256)
                         [:, 8 * h2:8 * h2 + 8, :],
                     in_=wqkv.rearrange("(k q) (g f) -> q k g f", q=128, f=256)
                         [:, 8 * h2:8 * h2 + 8, g, :]),
                    writes=[("wq", s, h2)], q=sWQ[s][h2])

         emit_wq(0)
         emit_wq(1)

         btile = SCR32.ap()[:, 3584:5632]
         op("scalar", lambda e: e.dma_start(
             out=btile, in_=bout.partition_broadcast(128)),
            writes=[("bt",)], q=sBR)

         def p1_load_stats(tt):
             p = tt % 4
             op(("sync", "scalar")[tt % 2],
                lambda e, p=p, tt=tt: e.dma_start(
                    out=xt_sl[p], in_=x_in[tt * 128:(tt + 1) * 128, :]),
                writes=XT_K[p], q=sX[p])

             st = stat.ap()[:, 8 * p:8 * p + 8]
             msum, m_, m2, varep, sq_, rstd, ssq = [st[:, i:i + 1] for i in range(7)]
             SK = [("stat", p)]

             op("vector", lambda e, p=p, msum=msum: e.reduce_sum(msum, xt_sl[p], AX.X),
                reads=XT_K[p], writes=SK)
             op("scalar", lambda e, p=p, ssq=ssq: e.activation(
                 xn_sl[p], xt_sl[p], AF.Square, accum_out=ssq),
                reads=XT_K[p], writes=XN_K[p] + SK)
             op("vector", lambda e, msum=msum, m_=m_: e.tensor_scalar(
                 m_, msum, 1.0 / 2048.0, None, ALU.mult), reads=SK, writes=SK)
             op("vector", lambda e, msum=msum, m2=m2: e.tensor_scalar(
                 m2, msum, msum, 1.0 / 2048.0 ** 2, ALU.mult, ALU.mult),
                reads=SK, writes=SK)
             op("vector", lambda e, ssq=ssq, m2=m2, varep=varep: e.scalar_tensor_tensor(
                 varep, ssq, 1.0 / 2048.0, m2, ALU.mult, ALU.subtract),
                reads=SK, writes=SK)
             op("scalar", lambda e, varep=varep, sq_=sq_: e.activation(
                 sq_, varep, AF.Sqrt, bias=epsc[:]), reads=SK + [("epsc",)], writes=SK)
             op("vector", lambda e, sq_=sq_, rstd=rstd: e.reciprocal(rstd, sq_),
                reads=SK, writes=SK)

         def p1_xn_transpose(tt):
             p = tt % 4
             st = stat.ap()[:, 8 * p:8 * p + 8]
             m_, rstd = st[:, 1:2], st[:, 5:6]
             SK = [("stat", p)]
             op("vector", lambda e, p=p, m_=m_, rstd=rstd: e.tensor_scalar(
                 xn_sl[p], xt_sl[p], m_, rstd, ALU.subtract, ALU.mult),
                reads=XT_K[p] + SK, writes=XN_K[p])

             half = tt // 4
             for j in range(4):
                 bank = PB[(tt * 4 + j) % 4]
                 bk = ("PT", (tt * 4 + j) % 4)

                 def trgroup(e, bank=bank, p=p, j=j):
                     inst = None
                     for i in range(4):
                         k = 4 * j + i
                         inst = e.transpose(bank[:, i * 128:(i + 1) * 128],
                                            xn_sl[p][:, k * 128:(k + 1) * 128], id32t[:])
                     return inst
                 op("tensor", trgroup, reads=XN_K[p] + [("id32",)], writes=[bk])

                 dst_xnT = AB.ap().rearrange("q (k t) -> q k t", k=16)[
                     :, 4 * j:4 * j + 4, tt * 128:(tt + 1) * 128]
                 src3 = bank.rearrange("q (k t) -> q k t", k=4)
                 dst_q1 = Q1T.ap().rearrange("q (k t) -> q k t", k=16)[
                     :, 4 * j:4 * j + 4, tt * 128:(tt + 1) * 128]
                 if tt % 2 == 0:
                     op("scalar", lambda e, dst=dst_xnT, src=src3: e.copy(dst, src),
                        reads=[bk], writes=[("ABh", half), ("rr",) + bk])
                     op("vector", lambda e, dst=dst_q1, bank=bank: e.tensor_scalar(
                         dst, bank.rearrange("q (k t) -> q k t", k=4), scol[:],
                         None, ALU.mult),
                        reads=[bk, ("scol",), ("rr",) + bk], writes=[("Q1h", half)])
                 else:
                     op("vector", lambda e, dst=dst_xnT, src=src3: e.tensor_copy(dst, src),
                        reads=[bk], writes=[("ABh", half), ("rr",) + bk])
                     op("scalar", lambda e, dst=dst_q1, bank=bank: e.activation(
                         dst, bank.rearrange("q (k t) -> q k t", k=4),
                         AF.Copy, scale=scol[:]),
                        reads=[bk, ("scol",), ("rr",) + bk], writes=[("Q1h", half)])

         p1_load_stats(0)
         p1_load_stats(1)
         for tt in range(4):
             p1_xn_transpose(tt)
             if tt + 2 < 8:
                 p1_load_stats(tt + 2)

         if ncut < 2:
             raise _Cut
         xnT3 = AB.ap().rearrange("q (k t) -> q k t", k=16)
         kv_view = {
             "k": KV1.ap().rearrange("q (g s H) -> q g s H", s=8, H=16),
             "v": VD.ap().rearrange("q (g s H) -> q g s H", s=8, H=16),
         }
         KV_KEYS = {("k", 0): ("K1A",), ("k", 1): ("K1B",),
                    ("v", 0): ("VDA",), ("v", 1): ("VDB",)}
         QTB = [[("QTb", b) for b in range(8 * hf, 8 * hf + 8)] for hf in range(2)]

         def p2_mm(ft, half):
             s = (ft // 2) % 2
             ftl = ft % 2
             bank = PB[(2 * ft + half) % 4]
             bk = ("PT", (2 * ft + half) % 4)

             def mmgroup(e, bank=bank, s=s, ftl=ftl, half=half):
                 inst = None
                 w3 = wq_slot[s].rearrange("q (k f) -> q k f", f=256)
                 for j in range(NK):
                     inst = e.matmul(
                         bank,
                         w3[:, j, ftl * 128:(ftl + 1) * 128],
                         xnT3[:, j, half * 512:(half + 1) * 512],
                         start=(j == 0), stop=(j == NK - 1))
                 return inst
             op("tensor", mmgroup,
                reads=[("wq", s, 0), ("wq", s, 1), ("ABh", half)], writes=[bk])

             if ft < 16:
                 h = ft
                 dstq = QT.ap().rearrange("q (g s H) -> q g s H", s=8, H=16)[
                     :, half * 64:(half + 1) * 64, :, h]
                 op("vector", lambda e, dstq=dstq, bank=bank: e.tensor_copy(
                     dstq, bank.rearrange("q (g s) -> q g s", s=8)),
                    reads=[bk], writes=QTB[half])
             else:
                 H = (ft - 16) % 16
                 which = "k" if ft < 32 else "v"
                 eng = "scalar"
                 dst = kv_view[which][:, half * 64:(half + 1) * 64, :, H]
                 wr = [KV_KEYS[(which, half)]]
                 if which == "v":
                     wr += [("VDq4", 2 * half), ("VDq4", 2 * half + 1)]
                 if H == 0:
                     if which == "v" and half == 0:
                         wr += [("xt", 0), ("xt", 1)]
                     elif which == "v" and half == 1:
                         wr += [("xn", 0), ("xn", 1)]
                     elif which == "k" and half == 0:
                         wr += [("xt", 2), ("xt", 3)]
                     else:
                         wr += [("xn", 2), ("xn", 3)]
                 op(eng, lambda e, dst=dst, bank=bank: e.copy(
                     dst, bank.rearrange("q (g s) -> q g s", s=8)),
                    reads=[bk], writes=wr)

         for ft in range(4):
             p1_xn_transpose(ft + 4)
             if ft + 6 < 8:
                 p1_load_stats(ft + 6)
             p2_mm(ft, 0)
             if ft % 2 == 1:
                 emit_wq(ft // 2 + 2)
         for ft in range(4, 48):
             p2_mm(ft, 0)
             p2_mm(ft, 1)
             if ft % 2 == 1 and ft // 2 + 2 < 24:
                 emit_wq(ft // 2 + 2)
         emit_wq(24)
         emit_wq(25)
         for ft in range(4):
             s = (ft // 2) % 2
             ftl = ft % 2
             bank = PB[(2 * ft + 1) % 4]
             bk = ("PT", (2 * ft + 1) % 4)

             def mmgroup(e, bank=bank, s=s, ftl=ftl):
                 inst = None
                 w3 = wq_slot[s].rearrange("q (k f) -> q k f", f=256)
                 for j in range(NK):
                     inst = e.matmul(
                         bank,
                         w3[:, j, ftl * 128:(ftl + 1) * 128],
                         xnT3[:, j, 512:1024],
                         start=(j == 0), stop=(j == NK - 1))
                 return inst
             op("tensor", mmgroup,
                reads=[("wq", s, 0), ("wq", s, 1), ("ABh", 1)], writes=[bk])
             dstq = QT.ap().rearrange("q (g s H) -> q g s H", s=8, H=16)[
                 :, 64:128, :, ft]
             op("vector", lambda e, dstq=dstq, bank=bank: e.tensor_copy(
                 dstq, bank.rearrange("q (g s) -> q g s", s=8)),
                reads=[bk], writes=QTB[1])

         if ncut < 3:
             raise _Cut
         NB2 = 16
         wm_sl = [SCRBF.ap()[:, i * 1024:(i + 1) * 1024] for i in range(2)]
         vp_sl = [SCRBF.ap()[:, 2048 + i * 1024:2048 + (i + 1) * 1024] for i in range(2)]
         wmn_sl = [SCRBF.ap()[:, 4096 + i * 1024:4096 + (i + 1) * 1024] for i in range(2)]
         wmt_sl = [SCRBF.ap()[:, 6144 + i * 1024:6144 + (i + 1) * 1024] for i in range(2)]
         WM_K = [("bf", i) for i in range(2)]
         VP_K = [("bf", 2 + i) for i in range(2)]
         WMN_K = [("bf", 4 + i) for i in range(2)]
         WMT_K = [("bf", 6 + i) for i in range(2)]
         mask_b = maskt.ap().unsqueeze(1).broadcast_to((128, 8, 128))
         SCB = [B0, B1]
         SCKs = [[("PT", 0), ("PT", 1)], [("B1",)]]

         def p3_tensor_front_vp(b):
             def vpgroup(e, b=b):
                 inst = None
                 for i in range(8):
                     g = 8 * b + i
                     inst = e.transpose(PVP.ap()[:, i * 128:(i + 1) * 128],
                                        VD.ap()[:, g * 128:(g + 1) * 128], idbft[:])
                 return inst
             vk = [("VDq4", b // 4), ("idbf",)]
             op("tensor", vpgroup, reads=vk, writes=[("PV",)])

         def p3_tensor_front_sc(b):
             pb = b % 2

             def scgroup(e, b=b, pb=pb):
                 inst = None
                 for i in range(8):
                     g = 8 * b + i
                     inst = e.matmul(SCB[pb].ap()[:, i * 128:(i + 1) * 128],
                                     QT.ap()[:, g * 128:(g + 1) * 128],
                                     KV1.ap()[:, g * 128:(g + 1) * 128],
                                     start=True, stop=True)
                 return inst
             op("tensor", scgroup,
                reads=[("K1A",) if b < 8 else ("K1B",), ("QTb", b)], writes=SCKs[pb])

         def p3_expmask(b):
             pb = b % 2
             if pb == 0:
                 op("vector", lambda e, pb=pb: e.tensor_copy(vp_sl[pb], PVP.ap()[:]),
                    reads=[("PV",)], writes=[VP_K[pb]])
             else:
                 op("scalar", lambda e, pb=pb: e.copy(vp_sl[pb], PVP.ap()[:]),
                    reads=[("PV",)], writes=[VP_K[pb]])
             op("scalar", lambda e, pb=pb: e.activation(
                 wm_sl[pb], SCB[pb].ap()[:], AF.Exp, scale=CSC),
                reads=SCKs[pb], writes=[WM_K[pb]])
             op("gpsimd", lambda e, pb=pb: e.tensor_tensor(
                 wm_sl[pb].rearrange("q (i m) -> q i m", i=8),
                 wm_sl[pb].rearrange("q (i m) -> q i m", i=8),
                 mask_b, ALU.mult),
                reads=[WM_K[pb], ("maskt",)], writes=[WM_K[pb]])

         def p3_consume_v(b):
             pb = b % 2
             dn = rscf.ap()[:, 16 * pb:16 * pb + 8]
             rq = rscf.ap()[:, 16 * pb + 8:16 * pb + 16]

             op("vector", lambda e, pb=pb, dn=dn: e.reduce_sum(
                 dn, wm_sl[pb].rearrange("q (i m) -> q i m", i=8), AX.X),
                reads=[WM_K[pb]], writes=[("dn", pb)])
             op("vector", lambda e, dn=dn, rq=rq: e.reciprocal(rq, dn),
                reads=[("dn", pb)], writes=[("rq", pb)])
             op("vector", lambda e, pb=pb, rq=rq: e.tensor_tensor(
                 wmn_sl[pb].rearrange("q (i m) -> q i m", i=8),
                 wm_sl[pb].rearrange("q (i m) -> q i m", i=8),
                 rq.unsqueeze(2).broadcast_to((128, 8, 128)), ALU.mult),
                reads=[WM_K[pb], ("rq", pb)], writes=[WMN_K[pb]])

         def p3_consume_t1(b):
             pb = b % 2

             def wtgroup(e, pb=pb):
                 inst = None
                 for i in range(8):
                     inst = e.transpose(PWT.ap()[:, i * 128:(i + 1) * 128],
                                        wmn_sl[pb][:, i * 128:(i + 1) * 128], idbft[:])
                 return inst
             op("tensor", wtgroup, reads=[WMN_K[pb], ("idbf",)], writes=[("PW",)])

             if pb == 0:
                 op("vector", lambda e, pb=pb: e.tensor_copy(wmt_sl[pb], PWT.ap()[:]),
                    reads=[("PW",)], writes=[WMT_K[pb]])
             else:
                 op("scalar", lambda e, pb=pb: e.copy(wmt_sl[pb], PWT.ap()[:]),
                    reads=[("PW",)], writes=[WMT_K[pb]])

         def p3_consume_t2(b):
             pb = b % 2

             def atgroup(e, pb=pb):
                 inst = None
                 for i in range(8):
                     inst = e.matmul(SCB[pb].ap()[:, i * 128:(i + 1) * 128],
                                     vp_sl[pb][:, i * 128:(i + 1) * 128],
                                     wmt_sl[pb][:, i * 128:(i + 1) * 128],
                                     start=True, stop=True)
                 return inst
             op("tensor", atgroup, reads=[VP_K[pb], WMT_K[pb]], writes=SCKs[pb])

             if pb == 0:
                 op("scalar", lambda e, b=b, pb=pb: e.copy(
                     QT.ap()[:, 1024 * b:1024 * b + 1024], SCB[pb].ap()[:]),
                    reads=SCKs[pb], writes=[("QTb", b)])
             else:
                 op("vector", lambda e, b=b, pb=pb: e.tensor_copy(
                     QT.ap()[:, 1024 * b:1024 * b + 1024], SCB[pb].ap()[:]),
                    reads=SCKs[pb], writes=[("QTb", b)])

         w4buf = [WQ.ap()[:, 0:8192], KV1.ap()[:, 0:8192],
                  KV1.ap()[:, 8192:16384], WQ.ap()[:, 0:8192]]
         W4CLAIM = [[("wq", 0, 0), ("wq", 0, 1), ("wq", 1, 0), ("wq", 1, 1)],
                    [("K1A",)], [("K1B",)], [("w4", 0)]]
         q2r = [VD.ap()[:, 4096 * i:4096 * (i + 1)].bitcast(F8)[:, 0:8192]
                for i in range(4)]

         def emit_wout(ncn):
             op("gpsimd", lambda e, ncn=ncn: e.dma_start(
                 out=w4buf[ncn].rearrange("q (k f) -> q k f", f=512),
                 in_=wout.rearrange("(k q) (n f) -> q k n f", q=128, f=512)
                     [:, :, ncn, :]),
                writes=[("w4", ncn)] + W4CLAIM[ncn], q=sWO[ncn % 2][ncn // 2])

         i2_sl = [SCR32.ap()[:, 5632:6656], SCR32.ap()[:, 6656:7680]]

         def emit_in2_piece(ncn, i):
             s = i % 2
             op("sync", lambda e, ncn=ncn, i=i, s=s: e.dma_start(
                 out=i2_sl[s].rearrange("q (k f) -> q k f", f=512),
                 in_=in2.rearrange("(k q) (n f) -> q k n f", q=128, f=512)
                     [:, 2 * i:2 * i + 2, ncn, :]),
                writes=[("i2p", s)], q=sIN[s])
             q23 = q2r[ncn].rearrange("q (k f) -> q k f", f=512)
             eng = "scalar" if i % 2 == 0 else "vector"
             if eng == "scalar":
                 op(eng, lambda e, i=i, s=s, q23=q23: e.activation(
                     q23[:, 2 * i:2 * i + 2, :],
                     i2_sl[s].rearrange("q (k f) -> q k f", f=512),
                     AF.Copy, scale=s1col[:]),
                    reads=[("i2p", s), ("s1col",)],
                    writes=[("VDq", ncn), ("VDq4", ncn)])
             else:
                 op(eng, lambda e, i=i, s=s, q23=q23: e.tensor_scalar(
                     q23[:, 2 * i:2 * i + 2, :],
                     i2_sl[s].rearrange("q (k f) -> q k f", f=512),
                     s1col[:], None, ALU.mult),
                    reads=[("i2p", s), ("s1col",)],
                    writes=[("VDq", ncn), ("VDq4", ncn)])

         out_sl = [SCR32.ap()[:, i * 512:(i + 1) * 512] for i in range(3)]
         OUT_K = [("o32", i) for i in range(3)]
         tmp_sl = [SCR32.ap()[:, 1536:2048], SCR32.ap()[:, 2048:2560]]
         TMP_K = [("tmp", 0), ("tmp", 1)]
         xre_sl = [SCR32.ap()[:, 2560:3072], SCR32.ap()[:, 3072:3584]]
         XRE_K = [("xre", 0), ("xre", 1)]

         at3 = QT.ap().rearrange("q (t h) -> q h t", h=16)
         q13 = Q1T.ap().rearrange("q (k t) -> q k t", k=16)

         def p4_tile(ncn, tt, bA, bF, bAk, bFk, pj, oi, tail=None, xring="sync"):
             op(xring, lambda e, tt=tt, ncn=ncn, pj=pj: e.dma_start(
                 out=xre_sl[pj],
                 in_=x_in[tt * 128:(tt + 1) * 128, ncn * 512:(ncn + 1) * 512]),
                writes=[XRE_K[pj]], q=sXR[pj])
             op("gpsimd", lambda e, ncn=ncn, pj=pj: e.tensor_tensor(
                 xre_sl[pj], xre_sl[pj],
                 btile[:, ncn * 512:(ncn + 1) * 512], ALU.add),
                reads=[XRE_K[pj], ("bt",)], writes=[XRE_K[pj]])

             def agroup(e, bA=bA, tt=tt, ncn=ncn):
                 inst = None
                 w43 = w4buf[ncn].rearrange("q (k n) -> q k n", n=512)
                 for k in range(NK):
                     inst = e.matmul(bA,
                                     at3[:, k, tt * 128:(tt + 1) * 128],
                                     w43[:, k, :],
                                     start=(k == 0), stop=(k == NK - 1))
                 return inst
             op("tensor", agroup,
                reads=[("QTb", 2 * tt), ("QTb", 2 * tt + 1), ("w4", ncn)],
                writes=[bAk])

             def fgroup(e, bF=bF, tt=tt, ncn=ncn):
                 inst = None
                 q23 = q2r[ncn].rearrange("p (k n) -> p k n", n=512)
                 for j2 in range(NK // 2):
                     inst = e.matmul(bF,
                                     q13[:, 2 * j2:2 * j2 + 2, tt * 128:(tt + 1) * 128],
                                     q23[:, 2 * j2:2 * j2 + 2, :],
                                     start=(j2 == 0), stop=(j2 == NK // 2 - 1),
                                     perf_mode=PM.DoubleRow)
                 return inst
             op("tensor", fgroup, reads=[("Q1h", tt // 4), ("VDq", ncn)], writes=[bFk])

             def epi():
                 osl = out_sl[oi]
                 op("vector", lambda e, pj=pj, bF=bF: e.scalar_tensor_tensor(
                     tmp_sl[pj], bF, ccol[:], xre_sl[pj],
                     ALU.mult, ALU.add),
                    reads=[bFk, ("ccol",), XRE_K[pj]], writes=[TMP_K[pj]])
                 op("vector", lambda e, osl=osl, pj=pj, bA=bA: e.tensor_tensor(
                     osl, tmp_sl[pj], bA, ALU.add),
                    reads=[TMP_K[pj], bAk], writes=[OUT_K[oi]])
                 op("scalar", lambda e, osl=osl, tt=tt, ncn=ncn: e.dma_start(
                     out=y_out[tt * 128:(tt + 1) * 128, ncn * 512:(ncn + 1) * 512],
                     in_=osl), reads=[OUT_K[oi]], q=sST[oi])
             if tail is None:
                 epi()
             else:
                 tail.append(epi)

         p4tail = []
         emit_wout(0)
         op("sync", lambda e: e.dma_start(
             out=btile, in_=bout.partition_broadcast(128)),
            writes=[("bt",)], q=sBR)

         stg = ncut == 31 or ncut == 4
         ilv = ncut == 32 or ncut == 4
         for b in range(NB2):
             if b > 1:
                 p3_consume_t1(b - 2)
             p3_tensor_front_vp(b)
             if b > 1:
                 p3_consume_t2(b - 2)
             p3_tensor_front_sc(b)
             p3_expmask(b)
             if b > 0:
                 p3_consume_v(b - 1)
             if stg and 4 <= b < 12:
                 ncn_p = (b - 4) // 4
                 emit_in2_piece(ncn_p, 2 * (b - 4 * ncn_p - 4))
                 emit_in2_piece(ncn_p, 2 * (b - 4 * ncn_p - 4) + 1)
             if stg and 12 <= b < 16:
                 emit_in2_piece(2, 2 * (b - 12))
                 emit_in2_piece(2, 2 * (b - 12) + 1)
             if stg and b == 8:
                 emit_wout(1)
             if ilv and 10 <= b:
                 t = b - 10
                 while p4tail:
                     p4tail.pop(0)()
                 p4_tile(0, t, PB[2], PB[3], ("PT", 2), ("PT", 3), t % 2, t % 3,
                         tail=p4tail, xring="scalar")
         p3_consume_v(NB2 - 1)
         for b in (NB2 - 2, NB2 - 1):
             p3_consume_t1(b)
             p3_consume_t2(b)
         while p4tail:
             p4tail.pop(0)()
         if ilv:
             p4_tile(0, 6, PB[2], PB[3], ("PT", 2), ("PT", 3), 0, 0, xring="scalar")
             p4_tile(0, 7, PB[2], PB[3], ("PT", 2), ("PT", 3), 1, 1, xring="scalar")
         if stg:
             emit_wout(2)

         if ncut != 4:
             raise _Cut
         for ncn in range((1 if ilv else 0), 4):
             for tt in range(NT):
                 j = ncn * NT + tt
                 pj = j % 2
                 p4_tile(ncn, tt, PB[2 * pj], PB[1 + 2 * pj],
                         ("PT", 2 * pj), ("PT", 1 + 2 * pj), pj, j % 3)
                 if ncn == 1 and tt < 8:
                     emit_in2_piece(3, tt)
             if ncn == 1:
                 emit_wout(3)

         for i in range(3):
             emit("sync", lambda e, s_=sST[i].h, v=sST[i].n: e.wait_ge(s_, v))

        try:
            _phases()
        except _Cut:
            op("gpsimd", lambda e: e.dma_start(out=y_out[0:128, :],
               in_=AB.ap().bitcast(F32)[:, 0:2048]), reads=[("ABh", 0)], q=sBR)
            emit("sync", lambda e, s_=sBR.h, v=sBR.n: e.wait_ge(s_, v))

        block = ec(nc.Block())

        @block.sync
        def _(e):
            for f in ops["sync"]:
                f(e)

        @block.gpsimd
        def _(e):
            for f in ops["gpsimd"]:
                f(e)

        @block.vector
        def _(e):
            for f in ops["vector"]:
                f(e)

        @block.scalar
        def _(e):
            for f in ops["scalar"]:
                f(e)

        @block.tensor
        def _(e):
            for f in ops["tensor"]:
                f(e)

    return nc


_NC_CACHE = None


def kernel(L0_attn_x, L0_attn_W_qkv, L0_attn_W_out, L0_attn_b_out,
           L0_ffn_input2, L0_ffn_scale, L0_ffn_scale1):
    global _NC_CACHE
    if _NC_CACHE is None:
        _NC_CACHE = build_program()
    nc = _NC_CACHE

    x = np.ascontiguousarray(np.asarray(L0_attn_x, dtype=np.float32).reshape(8192, 2048))
    wqkv = np.ascontiguousarray(np.asarray(L0_attn_W_qkv, dtype=np.float32))
    wout = np.ascontiguousarray(np.asarray(L0_attn_W_out, dtype=np.float32))
    bout = np.ascontiguousarray(np.asarray(L0_attn_b_out, dtype=np.float32).reshape(1, 2048))
    in2 = np.ascontiguousarray(np.asarray(L0_ffn_input2, dtype=np.float32))
    sc = np.asarray(L0_ffn_scale, dtype=np.float32).reshape(1, 1)
    sc1 = np.asarray(L0_ffn_scale1, dtype=np.float32).reshape(1, 1)

    id32 = np.eye(128, dtype=np.float32)
    idbf = np.eye(128, dtype=ml_dtypes.bfloat16)
    maskb = np.kron(np.eye(8, dtype=np.float32),
                    np.ones((16, 16), dtype=np.float32)).astype(ml_dtypes.bfloat16)

    in_maps = []
    for c in range(8):
        in_maps.append({
            "x": x[c * T:(c + 1) * T],
            "wqkv": wqkv, "wout": wout, "bout": bout, "in2": in2,
            "sc": sc, "sc1": sc1,
            "id32": id32, "idbf": idbf, "maskb": maskb,
        })

    res = run_bass_kernel_spmd(nc, in_maps, core_ids=list(range(8)))
    y = np.concatenate([r["y"] for r in res.results], axis=0)
    return y.reshape(4, 2048, 2048)


# version 33
# speedup vs baseline: 1.0712x; 1.0073x over previous
import numpy as np
import ml_dtypes
import contextlib
import concourse.bass as bass
import concourse.mybir as mybir
from concourse.bass_utils import run_bass_kernel_spmd

F32 = mybir.dt.float32
BF16 = mybir.dt.bfloat16
F8 = mybir.dt.float8e4
PM = mybir.MatmulPerfMode
AF = mybir.ActivationFunctionType
ALU = mybir.AluOpType
AX = mybir.AxisListType

T = 1024
NT = 8
NK = 16
NH = 16
NG = T // 8
NB = NG // 4
CSC = 1.0 / np.sqrt(128.0)

ENGS = ("sync", "gpsimd", "vector", "scalar", "tensor")


class Sem:
    def __init__(self, h):
        self.h = h
        self.n = 0

    def inc(self, k=1):
        self.n += k
        return self.n


class _Cut(Exception):
    pass


def build_program(ncut=4):
    nc = bass.Bass("TRN2", target_bir_lowering=False)

    x_in = nc.dram_tensor("x", [T, 2048], F32, kind="ExternalInput").ap()
    wqkv = nc.dram_tensor("wqkv", [2048, 6144], F32, kind="ExternalInput").ap()
    wout = nc.dram_tensor("wout", [2048, 2048], F32, kind="ExternalInput").ap()
    bout = nc.dram_tensor("bout", [1, 2048], F32, kind="ExternalInput").ap()
    in2 = nc.dram_tensor("in2", [2048, 2048], F32, kind="ExternalInput").ap()
    sc = nc.dram_tensor("sc", [1, 1], F32, kind="ExternalInput").ap()
    sc1 = nc.dram_tensor("sc1", [1, 1], F32, kind="ExternalInput").ap()
    id32 = nc.dram_tensor("id32", [128, 128], F32, kind="ExternalInput").ap()
    idbf = nc.dram_tensor("idbf", [128, 128], BF16, kind="ExternalInput").ap()
    maskb = nc.dram_tensor("maskb", [128, 128], BF16, kind="ExternalInput").ap()
    y_out = nc.dram_tensor("y", [T, 2048], F32, kind="ExternalOutput").ap()

    ops = {e: [] for e in ENGS}

    def emit(eng, fn):
        ops[eng].append(fn)

    es = contextlib.ExitStack()
    with es:
        ec = es.enter_context
        AB = ec(nc.sbuf_tensor("AB", [128, 16384], BF16))
        QT = ec(nc.sbuf_tensor("QT", [128, 16384], BF16))
        KV1 = ec(nc.sbuf_tensor("KV1", [128, 16384], BF16))
        VD = ec(nc.sbuf_tensor("VD", [128, 16384], BF16))
        Q1T = ec(nc.sbuf_tensor("Q1T", [128, 16384], F8))
        WQ = ec(nc.sbuf_tensor("WQ", [128, 8192], BF16))
        SCR32 = ec(nc.sbuf_tensor("SCR32", [128, 7680], F32))
        SCRBF = ec(nc.sbuf_tensor("SCRBF", [128, 8192], BF16))
        id32t = ec(nc.sbuf_tensor("id32t", [128, 128], F32))
        idbft = ec(nc.sbuf_tensor("idbft", [128, 128], BF16))
        maskt = ec(nc.sbuf_tensor("maskt", [128, 128], BF16))
        epsc = ec(nc.sbuf_tensor("epsc", [128, 1], F32))
        scol = ec(nc.sbuf_tensor("scol", [128, 1], F32))
        s1col = ec(nc.sbuf_tensor("s1col", [128, 1], F32))
        ccol = ec(nc.sbuf_tensor("ccol", [128, 1], F32))
        ctmp = ec(nc.sbuf_tensor("ctmp", [128, 2], F32))
        stat = ec(nc.sbuf_tensor("stat", [128, 32], F32))
        rscf = ec(nc.sbuf_tensor("rscf", [128, 32], F32))

        B0 = ec(nc.psum_tensor("B0", [128, 1024], F32))
        B1 = ec(nc.psum_tensor("B1", [128, 1024], F32))
        X0 = ec(nc.psum_tensor("X0", [128, 512], F32))
        X1 = ec(nc.psum_tensor("X1", [128, 512], F32))
        PVP = ec(nc.psum_tensor("PVP", [128, 1024], BF16))
        PWT = ec(nc.psum_tensor("PWT", [128, 1024], BF16))
        PB = [B0.ap()[:, 0:512], B0.ap()[:, 512:1024], X0.ap()[:], X1.ap()[:]]

        sC = Sem(ec(nc.semaphore("sC")))
        sX = [Sem(ec(nc.semaphore(f"sX{i}"))) for i in range(4)]
        sIN = [Sem(ec(nc.semaphore(f"sIN{i}"))) for i in range(2)]
        sXR = [Sem(ec(nc.semaphore(f"sXR{i}"))) for i in range(2)]
        sST = [Sem(ec(nc.semaphore(f"sST{i}"))) for i in range(3)]
        sWQ = [[Sem(ec(nc.semaphore(f"sWQ{i}{h}"))) for h in range(2)] for i in range(3)]
        sWO = [[Sem(ec(nc.semaphore(f"sWO{i}{h}"))) for h in range(2)] for i in range(2)]
        sBR = Sem(ec(nc.semaphore("sBR")))
        sT = Sem(ec(nc.semaphore("sT")))
        sV = Sem(ec(nc.semaphore("sV")))
        sS = Sem(ec(nc.semaphore("sS")))
        sG = Sem(ec(nc.semaphore("sG")))
        ESEM = {"vector": sV, "scalar": sS, "tensor": sT, "gpsimd": sG}

        last_w = {}
        readers = {}
        seen = {e: {} for e in ENGS}

        def _wait(eng, sem_, val):
            if val <= 0 or seen[eng].get(sem_, 0) >= val:
                return
            seen[eng][sem_] = val
            emit(eng, lambda e, s=sem_.h, v=val: e.wait_ge(s, v))

        def op(eng, fn, reads=(), writes=(), q=None):
            deps = {}
            for k in reads:
                w = last_w.get(k)
                if w:
                    deps[w[0]] = max(deps.get(w[0], 0), w[1])
            for k in writes:
                w = last_w.get(k)
                if w:
                    deps[w[0]] = max(deps.get(w[0], 0), w[1])
                for s_, v in readers.get(k, {}).items():
                    deps[s_] = max(deps.get(s_, 0), v)
            for s_, v in deps.items():
                _wait(eng, s_, v)
            sem_ = q if q is not None else ESEM[eng]
            amt = 16 if q is not None else 1
            emit(eng, lambda e, fn=fn, s=sem_.h, a=amt: fn(e).then_inc(s, a))
            t = sem_.inc(amt)
            tok = (sem_, t)
            for k in reads:
                readers.setdefault(k, {})[sem_] = t
            for k in writes:
                last_w[k] = tok
                readers[k] = {}
            return tok

        VDf = VD.ap().bitcast(F32)
        K1f = KV1.ap().bitcast(F32)
        xt_sl = [VDf[:, 0:2048], VDf[:, 2048:4096],
                 K1f[:, 0:2048], K1f[:, 2048:4096]]
        xn_sl = [VDf[:, 4096:6144], VDf[:, 6144:8192],
                 K1f[:, 4096:6144], K1f[:, 6144:8192]]
        XT_K = [[("xt", p)] for p in range(4)]
        XN_K = [[("xn", p)] for p in range(4)]
        BF_K = [("bf", i) for i in range(8)]

        def _phases():
         op("sync", lambda e: e.dma_start(out=id32t[:], in_=id32), writes=[("id32",)], q=sC)
         op("sync", lambda e: e.dma_start(out=idbft[:], in_=idbf), writes=[("idbf",)], q=sC)
         op("sync", lambda e: e.dma_start(out=maskt[:], in_=maskb), writes=[("maskt",)], q=sC)
         op("sync", lambda e: e.dma_start(out=scol[:], in_=sc.partition_broadcast(128)),
            writes=[("scol",)], q=sC)
         op("sync", lambda e: e.dma_start(out=s1col[:], in_=sc1.partition_broadcast(128)),
            writes=[("s1col",)], q=sC)
         for _k in (("id32",), ("idbf",), ("maskt",), ("scol",), ("s1col",)):
             last_w[_k] = (sC, sC.n)
         op("vector", lambda e: e.memset(epsc[:], 1e-5), writes=[("epsc",)])
         op("vector", lambda e: e.reciprocal(ctmp[:, 0:1], scol[:]),
            reads=[("scol",)], writes=[("ctmp",)])
         op("vector", lambda e: e.reciprocal(ctmp[:, 1:2], s1col[:]),
            reads=[("s1col",), ("ctmp",)], writes=[("ctmp",)])
         op("vector", lambda e: e.tensor_tensor(ccol[:], ctmp[:, 0:1], ctmp[:, 1:2], ALU.mult),
            reads=[("ctmp",)], writes=[("ccol",)])

         wq_slot = [WQ.ap()[:, 0:4096], WQ.ap()[:, 4096:8192],
                    SCRBF.ap()[:, 0:4096]]
         WQX = [[], [], [("bf", 0), ("bf", 1), ("bf", 2), ("bf", 3)]]

         def emit_wq(p):
             s = p % 3
             g = p % 24
             for h2 in range(2):
                 op("gpsimd", lambda e, g=g, s=s, h2=h2: e.dma_start(
                     out=wq_slot[s].rearrange("q (k f) -> q k f", f=256)
                         [:, 8 * h2:8 * h2 + 8, :],
                     in_=wqkv.rearrange("(k q) (g f) -> q k g f", q=128, f=256)
                         [:, 8 * h2:8 * h2 + 8, g, :]),
                    writes=[("wq", s, h2)] + (WQX[s] if h2 == 0 else []),
                    q=sWQ[s][h2])

         emit_wq(0)
         emit_wq(1)
         emit_wq(2)

         btile = SCR32.ap()[:, 3584:5632]
         op("scalar", lambda e: e.dma_start(
             out=btile, in_=bout.partition_broadcast(128)),
            writes=[("bt",)], q=sBR)

         def p1_load_stats(tt):
             p = tt % 4
             op(("sync", "scalar")[tt % 2],
                lambda e, p=p, tt=tt: e.dma_start(
                    out=xt_sl[p], in_=x_in[tt * 128:(tt + 1) * 128, :]),
                writes=XT_K[p], q=sX[p])

             st = stat.ap()[:, 8 * p:8 * p + 8]
             msum, m_, m2, varep, sq_, rstd, ssq = [st[:, i:i + 1] for i in range(7)]
             SK = [("stat", p)]

             op("vector", lambda e, p=p, msum=msum: e.reduce_sum(msum, xt_sl[p], AX.X),
                reads=XT_K[p], writes=SK)
             op("scalar", lambda e, p=p, ssq=ssq: e.activation(
                 xn_sl[p], xt_sl[p], AF.Square, accum_out=ssq),
                reads=XT_K[p], writes=XN_K[p] + SK)
             op("vector", lambda e, msum=msum, m_=m_: e.tensor_scalar(
                 m_, msum, 1.0 / 2048.0, None, ALU.mult), reads=SK, writes=SK)
             op("vector", lambda e, msum=msum, m2=m2: e.tensor_scalar(
                 m2, msum, msum, 1.0 / 2048.0 ** 2, ALU.mult, ALU.mult),
                reads=SK, writes=SK)
             op("vector", lambda e, ssq=ssq, m2=m2, varep=varep: e.scalar_tensor_tensor(
                 varep, ssq, 1.0 / 2048.0, m2, ALU.mult, ALU.subtract),
                reads=SK, writes=SK)
             op("scalar", lambda e, varep=varep, sq_=sq_: e.activation(
                 sq_, varep, AF.Sqrt, bias=epsc[:]), reads=SK + [("epsc",)], writes=SK)
             op("vector", lambda e, sq_=sq_, rstd=rstd: e.reciprocal(rstd, sq_),
                reads=SK, writes=SK)

         def p1_xn_transpose(tt):
             p = tt % 4
             st = stat.ap()[:, 8 * p:8 * p + 8]
             m_, rstd = st[:, 1:2], st[:, 5:6]
             SK = [("stat", p)]
             op("vector", lambda e, p=p, m_=m_, rstd=rstd: e.tensor_scalar(
                 xn_sl[p], xt_sl[p], m_, rstd, ALU.subtract, ALU.mult),
                reads=XT_K[p] + SK, writes=XN_K[p])

             half = tt // 4
             for j in range(4):
                 bank = PB[(tt * 4 + j) % 4]
                 bk = ("PT", (tt * 4 + j) % 4)

                 def trgroup(e, bank=bank, p=p, j=j):
                     inst = None
                     for i in range(4):
                         k = 4 * j + i
                         inst = e.transpose(bank[:, i * 128:(i + 1) * 128],
                                            xn_sl[p][:, k * 128:(k + 1) * 128], id32t[:])
                     return inst
                 op("tensor", trgroup, reads=XN_K[p] + [("id32",)], writes=[bk])

                 dst_xnT = AB.ap().rearrange("q (k t) -> q k t", k=16)[
                     :, 4 * j:4 * j + 4, tt * 128:(tt + 1) * 128]
                 src3 = bank.rearrange("q (k t) -> q k t", k=4)
                 dst_q1 = Q1T.ap().rearrange("q (k t) -> q k t", k=16)[
                     :, 4 * j:4 * j + 4, tt * 128:(tt + 1) * 128]
                 if tt % 2 == 0:
                     op("scalar", lambda e, dst=dst_xnT, src=src3: e.copy(dst, src),
                        reads=[bk], writes=[("ABh", half), ("rr",) + bk])
                     op("vector", lambda e, dst=dst_q1, bank=bank: e.tensor_scalar(
                         dst, bank.rearrange("q (k t) -> q k t", k=4), scol[:],
                         None, ALU.mult),
                        reads=[bk, ("scol",), ("rr",) + bk], writes=[("Q1h", half)])
                 else:
                     op("vector", lambda e, dst=dst_xnT, src=src3: e.tensor_copy(dst, src),
                        reads=[bk], writes=[("ABh", half), ("rr",) + bk])
                     op("scalar", lambda e, dst=dst_q1, bank=bank: e.activation(
                         dst, bank.rearrange("q (k t) -> q k t", k=4),
                         AF.Copy, scale=scol[:]),
                        reads=[bk, ("scol",), ("rr",) + bk], writes=[("Q1h", half)])

         p1_load_stats(0)
         p1_load_stats(1)
         for tt in range(4):
             p1_xn_transpose(tt)
             if tt + 2 < 8:
                 p1_load_stats(tt + 2)

         if ncut < 2:
             raise _Cut
         xnT3 = AB.ap().rearrange("q (k t) -> q k t", k=16)
         kv_view = {
             "k": KV1.ap().rearrange("q (g s H) -> q g s H", s=8, H=16),
             "v": VD.ap().rearrange("q (g s H) -> q g s H", s=8, H=16),
         }
         KV_KEYS = {("k", 0): ("K1A",), ("k", 1): ("K1B",),
                    ("v", 0): ("VDA",), ("v", 1): ("VDB",)}
         QTB = [[("QTb", b) for b in range(8 * hf, 8 * hf + 8)] for hf in range(2)]

         def p2_mm(ft, half):
             s = (ft // 2) % 3
             ftl = ft % 2
             bank = PB[(2 * ft + half) % 4]
             bk = ("PT", (2 * ft + half) % 4)

             def mmgroup(e, bank=bank, s=s, ftl=ftl, half=half):
                 inst = None
                 w3 = wq_slot[s].rearrange("q (k f) -> q k f", f=256)
                 for j in range(NK):
                     inst = e.matmul(
                         bank,
                         w3[:, j, ftl * 128:(ftl + 1) * 128],
                         xnT3[:, j, half * 512:(half + 1) * 512],
                         start=(j == 0), stop=(j == NK - 1))
                 return inst
             op("tensor", mmgroup,
                reads=[("wq", s, 0), ("wq", s, 1), ("ABh", half)], writes=[bk])

             if ft < 16:
                 h = ft
                 dstq = QT.ap().rearrange("q (g s H) -> q g s H", s=8, H=16)[
                     :, half * 64:(half + 1) * 64, :, h]
                 op("vector", lambda e, dstq=dstq, bank=bank: e.tensor_copy(
                     dstq, bank.rearrange("q (g s) -> q g s", s=8)),
                    reads=[bk], writes=QTB[half])
             else:
                 H = (ft - 16) % 16
                 which = "k" if ft < 32 else "v"
                 eng = "scalar"
                 dst = kv_view[which][:, half * 64:(half + 1) * 64, :, H]
                 wr = [KV_KEYS[(which, half)]]
                 if which == "v":
                     wr += [("VDq4", 2 * half), ("VDq4", 2 * half + 1)]
                 if H == 0:
                     if which == "v" and half == 0:
                         wr += [("xt", 0), ("xt", 1)]
                     elif which == "v" and half == 1:
                         wr += [("xn", 0), ("xn", 1)]
                     elif which == "k" and half == 0:
                         wr += [("xt", 2), ("xt", 3)]
                     else:
                         wr += [("xn", 2), ("xn", 3)]
                 op(eng, lambda e, dst=dst, bank=bank: e.copy(
                     dst, bank.rearrange("q (g s) -> q g s", s=8)),
                    reads=[bk], writes=wr)

         for ft in range(4):
             p1_xn_transpose(ft + 4)
             if ft + 6 < 8:
                 p1_load_stats(ft + 6)
             p2_mm(ft, 0)
             if ft % 2 == 1:
                 emit_wq(ft // 2 + 3)
         for ft in range(4, 48):
             p2_mm(ft, 0)
             p2_mm(ft, 1)
             if ft % 2 == 1 and ft // 2 + 3 < 24:
                 emit_wq(ft // 2 + 3)
         emit_wq(24)
         emit_wq(25)
         for ft in range(4):
             s = (ft // 2) % 3
             ftl = ft % 2
             bank = PB[(2 * ft + 1) % 4]
             bk = ("PT", (2 * ft + 1) % 4)

             def mmgroup(e, bank=bank, s=s, ftl=ftl):
                 inst = None
                 w3 = wq_slot[s].rearrange("q (k f) -> q k f", f=256)
                 for j in range(NK):
                     inst = e.matmul(
                         bank,
                         w3[:, j, ftl * 128:(ftl + 1) * 128],
                         xnT3[:, j, 512:1024],
                         start=(j == 0), stop=(j == NK - 1))
                 return inst
             op("tensor", mmgroup,
                reads=[("wq", s, 0), ("wq", s, 1), ("ABh", 1)], writes=[bk])
             dstq = QT.ap().rearrange("q (g s H) -> q g s H", s=8, H=16)[
                 :, 64:128, :, ft]
             op("vector", lambda e, dstq=dstq, bank=bank: e.tensor_copy(
                 dstq, bank.rearrange("q (g s) -> q g s", s=8)),
                reads=[bk], writes=QTB[1])

         if ncut < 3:
             raise _Cut
         NB2 = 16
         wm_sl = [SCRBF.ap()[:, i * 1024:(i + 1) * 1024] for i in range(2)]
         vp_sl = [SCRBF.ap()[:, 2048 + i * 1024:2048 + (i + 1) * 1024] for i in range(2)]
         wmn_sl = [SCRBF.ap()[:, 4096 + i * 1024:4096 + (i + 1) * 1024] for i in range(2)]
         wmt_sl = [SCRBF.ap()[:, 6144 + i * 1024:6144 + (i + 1) * 1024] for i in range(2)]
         WM_K = [("bf", i) for i in range(2)]
         VP_K = [("bf", 2 + i) for i in range(2)]
         WMN_K = [("bf", 4 + i) for i in range(2)]
         WMT_K = [("bf", 6 + i) for i in range(2)]
         mask_b = maskt.ap().unsqueeze(1).broadcast_to((128, 8, 128))
         SCB = [B0, B1]
         SCKs = [[("PT", 0), ("PT", 1)], [("B1",)]]

         def p3_tensor_front_vp(b):
             def vpgroup(e, b=b):
                 inst = None
                 for i in range(8):
                     g = 8 * b + i
                     inst = e.transpose(PVP.ap()[:, i * 128:(i + 1) * 128],
                                        VD.ap()[:, g * 128:(g + 1) * 128], idbft[:])
                 return inst
             vk = [("VDq4", b // 4), ("idbf",)]
             op("tensor", vpgroup, reads=vk, writes=[("PV",)])

         def p3_tensor_front_sc(b):
             pb = b % 2

             def scgroup(e, b=b, pb=pb):
                 inst = None
                 for i in range(8):
                     g = 8 * b + i
                     inst = e.matmul(SCB[pb].ap()[:, i * 128:(i + 1) * 128],
                                     QT.ap()[:, g * 128:(g + 1) * 128],
                                     KV1.ap()[:, g * 128:(g + 1) * 128],
                                     start=True, stop=True)
                 return inst
             op("tensor", scgroup,
                reads=[("K1A",) if b < 8 else ("K1B",), ("QTb", b)], writes=SCKs[pb])

         def p3_expmask(b):
             pb = b % 2
             if pb == 0:
                 op("vector", lambda e, pb=pb: e.tensor_copy(vp_sl[pb], PVP.ap()[:]),
                    reads=[("PV",)], writes=[VP_K[pb]])
             else:
                 op("scalar", lambda e, pb=pb: e.copy(vp_sl[pb], PVP.ap()[:]),
                    reads=[("PV",)], writes=[VP_K[pb]])
             op("scalar", lambda e, pb=pb: e.activation(
                 wm_sl[pb], SCB[pb].ap()[:], AF.Exp, scale=CSC),
                reads=SCKs[pb], writes=[WM_K[pb]])
             op("gpsimd", lambda e, pb=pb: e.tensor_tensor(
                 wm_sl[pb].rearrange("q (i m) -> q i m", i=8),
                 wm_sl[pb].rearrange("q (i m) -> q i m", i=8),
                 mask_b, ALU.mult),
                reads=[WM_K[pb], ("maskt",)], writes=[WM_K[pb]])

         def p3_consume_v(b):
             pb = b % 2
             dn = rscf.ap()[:, 16 * pb:16 * pb + 8]
             rq = rscf.ap()[:, 16 * pb + 8:16 * pb + 16]

             op("vector", lambda e, pb=pb, dn=dn: e.reduce_sum(
                 dn, wm_sl[pb].rearrange("q (i m) -> q i m", i=8), AX.X),
                reads=[WM_K[pb]], writes=[("dn", pb)])
             op("vector", lambda e, dn=dn, rq=rq: e.reciprocal(rq, dn),
                reads=[("dn", pb)], writes=[("rq", pb)])
             op("vector", lambda e, pb=pb, rq=rq: e.tensor_tensor(
                 wmn_sl[pb].rearrange("q (i m) -> q i m", i=8),
                 wm_sl[pb].rearrange("q (i m) -> q i m", i=8),
                 rq.unsqueeze(2).broadcast_to((128, 8, 128)), ALU.mult),
                reads=[WM_K[pb], ("rq", pb)], writes=[WMN_K[pb]])

         def p3_consume_t1(b):
             pb = b % 2

             def wtgroup(e, pb=pb):
                 inst = None
                 for i in range(8):
                     inst = e.transpose(PWT.ap()[:, i * 128:(i + 1) * 128],
                                        wmn_sl[pb][:, i * 128:(i + 1) * 128], idbft[:])
                 return inst
             op("tensor", wtgroup, reads=[WMN_K[pb], ("idbf",)], writes=[("PW",)])

             if pb == 0:
                 op("vector", lambda e, pb=pb: e.tensor_copy(wmt_sl[pb], PWT.ap()[:]),
                    reads=[("PW",)], writes=[WMT_K[pb]])
             else:
                 op("scalar", lambda e, pb=pb: e.copy(wmt_sl[pb], PWT.ap()[:]),
                    reads=[("PW",)], writes=[WMT_K[pb]])

         def p3_consume_t2(b):
             pb = b % 2

             def atgroup(e, pb=pb):
                 inst = None
                 for i in range(8):
                     inst = e.matmul(SCB[pb].ap()[:, i * 128:(i + 1) * 128],
                                     vp_sl[pb][:, i * 128:(i + 1) * 128],
                                     wmt_sl[pb][:, i * 128:(i + 1) * 128],
                                     start=True, stop=True)
                 return inst
             op("tensor", atgroup, reads=[VP_K[pb], WMT_K[pb]], writes=SCKs[pb])

             if pb == 0:
                 op("scalar", lambda e, b=b, pb=pb: e.copy(
                     QT.ap()[:, 1024 * b:1024 * b + 1024], SCB[pb].ap()[:]),
                    reads=SCKs[pb], writes=[("QTb", b)])
             else:
                 op("vector", lambda e, b=b, pb=pb: e.tensor_copy(
                     QT.ap()[:, 1024 * b:1024 * b + 1024], SCB[pb].ap()[:]),
                    reads=SCKs[pb], writes=[("QTb", b)])

         w4buf = [WQ.ap()[:, 0:8192], KV1.ap()[:, 0:8192],
                  KV1.ap()[:, 8192:16384], WQ.ap()[:, 0:8192]]
         W4CLAIM = [[("wq", 0, 0), ("wq", 0, 1), ("wq", 1, 0), ("wq", 1, 1)],
                    [("K1A",)], [("K1B",)], [("w4", 0)]]
         q2r = [VD.ap()[:, 4096 * i:4096 * (i + 1)].bitcast(F8)[:, 0:8192]
                for i in range(4)]

         def emit_wout(ncn):
             op("gpsimd", lambda e, ncn=ncn: e.dma_start(
                 out=w4buf[ncn].rearrange("q (k f) -> q k f", f=512),
                 in_=wout.rearrange("(k q) (n f) -> q k n f", q=128, f=512)
                     [:, :, ncn, :]),
                writes=[("w4", ncn)] + W4CLAIM[ncn], q=sWO[ncn % 2][ncn // 2])

         i2_sl = [SCR32.ap()[:, 5632:6656], SCR32.ap()[:, 6656:7680]]

         def emit_in2_piece(ncn, i):
             s = i % 2
             op("sync", lambda e, ncn=ncn, i=i, s=s: e.dma_start(
                 out=i2_sl[s].rearrange("q (k f) -> q k f", f=512),
                 in_=in2.rearrange("(k q) (n f) -> q k n f", q=128, f=512)
                     [:, 2 * i:2 * i + 2, ncn, :]),
                writes=[("i2p", s)], q=sIN[s])
             q23 = q2r[ncn].rearrange("q (k f) -> q k f", f=512)
             eng = "scalar" if i % 2 == 0 else "vector"
             if eng == "scalar":
                 op(eng, lambda e, i=i, s=s, q23=q23: e.activation(
                     q23[:, 2 * i:2 * i + 2, :],
                     i2_sl[s].rearrange("q (k f) -> q k f", f=512),
                     AF.Copy, scale=s1col[:]),
                    reads=[("i2p", s), ("s1col",)],
                    writes=[("VDq", ncn), ("VDq4", ncn)])
             else:
                 op(eng, lambda e, i=i, s=s, q23=q23: e.tensor_scalar(
                     q23[:, 2 * i:2 * i + 2, :],
                     i2_sl[s].rearrange("q (k f) -> q k f", f=512),
                     s1col[:], None, ALU.mult),
                    reads=[("i2p", s), ("s1col",)],
                    writes=[("VDq", ncn), ("VDq4", ncn)])

         out_sl = [SCR32.ap()[:, i * 512:(i + 1) * 512] for i in range(3)]
         OUT_K = [("o32", i) for i in range(3)]
         tmp_sl = [SCR32.ap()[:, 1536:2048], SCR32.ap()[:, 2048:2560]]
         TMP_K = [("tmp", 0), ("tmp", 1)]
         xre_sl = [SCR32.ap()[:, 2560:3072], SCR32.ap()[:, 3072:3584]]
         XRE_K = [("xre", 0), ("xre", 1)]

         at3 = QT.ap().rearrange("q (t h) -> q h t", h=16)
         q13 = Q1T.ap().rearrange("q (k t) -> q k t", k=16)

         def p4_tile(ncn, tt, bA, bF, bAk, bFk, pj, oi, tail=None, xring="sync"):
             op(xring, lambda e, tt=tt, ncn=ncn, pj=pj: e.dma_start(
                 out=xre_sl[pj],
                 in_=x_in[tt * 128:(tt + 1) * 128, ncn * 512:(ncn + 1) * 512]),
                writes=[XRE_K[pj]], q=sXR[pj])
             op("gpsimd", lambda e, ncn=ncn, pj=pj: e.tensor_tensor(
                 xre_sl[pj], xre_sl[pj],
                 btile[:, ncn * 512:(ncn + 1) * 512], ALU.add),
                reads=[XRE_K[pj], ("bt",)], writes=[XRE_K[pj]])

             def agroup(e, bA=bA, tt=tt, ncn=ncn):
                 inst = None
                 w43 = w4buf[ncn].rearrange("q (k n) -> q k n", n=512)
                 for k in range(NK):
                     inst = e.matmul(bA,
                                     at3[:, k, tt * 128:(tt + 1) * 128],
                                     w43[:, k, :],
                                     start=(k == 0), stop=(k == NK - 1))
                 return inst
             op("tensor", agroup,
                reads=[("QTb", 2 * tt), ("QTb", 2 * tt + 1), ("w4", ncn)],
                writes=[bAk])

             def fgroup(e, bF=bF, tt=tt, ncn=ncn):
                 inst = None
                 q23 = q2r[ncn].rearrange("p (k n) -> p k n", n=512)
                 for j2 in range(NK // 2):
                     inst = e.matmul(bF,
                                     q13[:, 2 * j2:2 * j2 + 2, tt * 128:(tt + 1) * 128],
                                     q23[:, 2 * j2:2 * j2 + 2, :],
                                     start=(j2 == 0), stop=(j2 == NK // 2 - 1),
                                     perf_mode=PM.DoubleRow)
                 return inst
             op("tensor", fgroup, reads=[("Q1h", tt // 4), ("VDq", ncn)], writes=[bFk])

             def epi():
                 osl = out_sl[oi]
                 op("vector", lambda e, pj=pj, bF=bF: e.scalar_tensor_tensor(
                     tmp_sl[pj], bF, ccol[:], xre_sl[pj],
                     ALU.mult, ALU.add),
                    reads=[bFk, ("ccol",), XRE_K[pj]], writes=[TMP_K[pj]])
                 op("vector", lambda e, osl=osl, pj=pj, bA=bA: e.tensor_tensor(
                     osl, tmp_sl[pj], bA, ALU.add),
                    reads=[TMP_K[pj], bAk], writes=[OUT_K[oi]])
                 op("scalar", lambda e, osl=osl, tt=tt, ncn=ncn: e.dma_start(
                     out=y_out[tt * 128:(tt + 1) * 128, ncn * 512:(ncn + 1) * 512],
                     in_=osl), reads=[OUT_K[oi]], q=sST[oi])
             if tail is None:
                 epi()
             else:
                 tail.append(epi)

         p4tail = []
         emit_wout(0)
         op("sync", lambda e: e.dma_start(
             out=btile, in_=bout.partition_broadcast(128)),
            writes=[("bt",)], q=sBR)

         stg = ncut == 31 or ncut == 4
         ilv = ncut == 32 or ncut == 4
         for b in range(NB2):
             if b > 1:
                 p3_consume_t1(b - 2)
             p3_tensor_front_vp(b)
             if b > 1:
                 p3_consume_t2(b - 2)
             p3_tensor_front_sc(b)
             p3_expmask(b)
             if b > 0:
                 p3_consume_v(b - 1)
             if stg and 4 <= b < 12:
                 ncn_p = (b - 4) // 4
                 emit_in2_piece(ncn_p, 2 * (b - 4 * ncn_p - 4))
                 emit_in2_piece(ncn_p, 2 * (b - 4 * ncn_p - 4) + 1)
             if stg and 12 <= b < 16:
                 emit_in2_piece(2, 2 * (b - 12))
                 emit_in2_piece(2, 2 * (b - 12) + 1)
             if stg and b == 8:
                 emit_wout(1)
             if stg and b == 15:
                 emit_wout(2)
             if ilv and 10 <= b:
                 t = b - 10
                 while p4tail:
                     p4tail.pop(0)()
                 p4_tile(0, t, PB[2], PB[3], ("PT", 2), ("PT", 3), t % 2, t % 3,
                         tail=p4tail, xring="scalar")
         p3_consume_v(NB2 - 1)
         for b in (NB2 - 2, NB2 - 1):
             p3_consume_t1(b)
             p3_consume_t2(b)
         while p4tail:
             p4tail.pop(0)()
         if ilv:
             p4_tile(0, 6, PB[2], PB[3], ("PT", 2), ("PT", 3), 0, 0, xring="scalar")
             p4_tile(0, 7, PB[2], PB[3], ("PT", 2), ("PT", 3), 1, 1, xring="scalar")
         if ncut != 4:
             raise _Cut
         for ncn in range((1 if ilv else 0), 4):
             for tt in range(NT):
                 j = ncn * NT + tt
                 pj = j % 2
                 p4_tile(ncn, tt, PB[2 * pj], PB[1 + 2 * pj],
                         ("PT", 2 * pj), ("PT", 1 + 2 * pj), pj, j % 3)
                 if ncn == 1 and tt < 8:
                     emit_in2_piece(3, tt)
             if ncn == 1:
                 emit_wout(3)

         for i in range(3):
             emit("sync", lambda e, s_=sST[i].h, v=sST[i].n: e.wait_ge(s_, v))

        try:
            _phases()
        except _Cut:
            op("gpsimd", lambda e: e.dma_start(out=y_out[0:128, :],
               in_=AB.ap().bitcast(F32)[:, 0:2048]), reads=[("ABh", 0)], q=sBR)
            emit("sync", lambda e, s_=sBR.h, v=sBR.n: e.wait_ge(s_, v))

        block = ec(nc.Block())

        @block.sync
        def _(e):
            for f in ops["sync"]:
                f(e)

        @block.gpsimd
        def _(e):
            for f in ops["gpsimd"]:
                f(e)

        @block.vector
        def _(e):
            for f in ops["vector"]:
                f(e)

        @block.scalar
        def _(e):
            for f in ops["scalar"]:
                f(e)

        @block.tensor
        def _(e):
            for f in ops["tensor"]:
                f(e)

    return nc


_NC_CACHE = None


def kernel(L0_attn_x, L0_attn_W_qkv, L0_attn_W_out, L0_attn_b_out,
           L0_ffn_input2, L0_ffn_scale, L0_ffn_scale1):
    global _NC_CACHE
    if _NC_CACHE is None:
        _NC_CACHE = build_program()
    nc = _NC_CACHE

    x = np.ascontiguousarray(np.asarray(L0_attn_x, dtype=np.float32).reshape(8192, 2048))
    wqkv = np.ascontiguousarray(np.asarray(L0_attn_W_qkv, dtype=np.float32))
    wout = np.ascontiguousarray(np.asarray(L0_attn_W_out, dtype=np.float32))
    bout = np.ascontiguousarray(np.asarray(L0_attn_b_out, dtype=np.float32).reshape(1, 2048))
    in2 = np.ascontiguousarray(np.asarray(L0_ffn_input2, dtype=np.float32))
    sc = np.asarray(L0_ffn_scale, dtype=np.float32).reshape(1, 1)
    sc1 = np.asarray(L0_ffn_scale1, dtype=np.float32).reshape(1, 1)

    id32 = np.eye(128, dtype=np.float32)
    idbf = np.eye(128, dtype=ml_dtypes.bfloat16)
    maskb = np.kron(np.eye(8, dtype=np.float32),
                    np.ones((16, 16), dtype=np.float32)).astype(ml_dtypes.bfloat16)

    in_maps = []
    for c in range(8):
        in_maps.append({
            "x": x[c * T:(c + 1) * T],
            "wqkv": wqkv, "wout": wout, "bout": bout, "in2": in2,
            "sc": sc, "sc1": sc1,
            "id32": id32, "idbf": idbf, "maskb": maskb,
        })

    res = run_bass_kernel_spmd(nc, in_maps, core_ids=list(range(8)))
    y = np.concatenate([r["y"] for r in res.results], axis=0)
    return y.reshape(4, 2048, 2048)
